# version 05
# baseline (speedup 1.0000x reference)
import numpy as np
import ml_dtypes
import concourse.bass as bass
import concourse.mybir as mybir
from concourse.bass_utils import run_bass_kernel_spmd

F32 = mybir.dt.float32
BF16 = mybir.dt.bfloat16
FP8 = mybir.dt.float8e4
AF = mybir.ActivationFunctionType
OP = mybir.AluOpType
AX = mybir.AxisListType

C = 2048
NTOK = 2048
H = 16
HID = 8192
GBIG = float(2 ** 40)
SC_ATT = float(128) ** -0.5
CAS_K = [128.0, 64.0, 32.0, 16.0, 8.0, 4.0, 2.0, 1.0]
CAS_TAU = [255.0, 127.0, 63.0, 31.0, 15.0, 7.0, 3.0, 1.0]

ENGS = ("sync", "scalar", "vector", "tensor", "gpsimd")

PC_MQ, PC_MG, PC_M1, PC_M2, PC_IM1, PC_IM2, PC_MP = 0, 1, 2, 3, 4, 5, 56
PC_XSC1 = 6
PC_XSC2 = 24
PC_XSC3 = 40

KT_K, KT_V, KT_KV0 = 0, 128, 256
UB_QP, UB_KP, UB_VP, UB_VT = 0, 512, 1024, 1536
UB_B = 0
UB_LB = 2048
UB_EEX, UB_AT = 4096, 4608
UB_MASK = 5120


class Sem:
    def __init__(self, h, name):
        self.h, self.name, self.n = h, name, 0

    def inc(self, k=1):
        self.n += k
        return self.n


class MK:
    def __init__(self, dbg=None):
        self.dbg = dbg
        self.nc = bass.Bass("TRN2", target_bir_lowering=False)
        self.nc.detect_race_conditions = False
        self.ops = {e: [] for e in ENGS}
        self._ctx = []
        self.last = {}

    def op(self, eng, emit, need=(), tag=None, inc=None, inc_n=1,
           drain=False):
        waits = {}
        for t in need:
            if t in self.last:
                s, v = self.last[t]
                if waits.get(s.name, (None, -1))[1] < v:
                    waits[s.name] = (s, v)
        if inc is not None:
            val = inc.inc(inc_n)
            if tag is not None:
                self.last[tag] = (inc, val)
        wl = tuple((s.h, v) for (s, v) in waits.values())

        def run(e, emit=emit, wl=wl, inc=inc, inc_n=inc_n, drain=drain):
            for h, v in wl:
                e.wait_ge(h, v)
            if emit is None:
                return
            inst = emit(e)
            if drain:
                inst = e.drain()
            if inc is not None:
                inst.then_inc(inc.h, inc_n)
        self.ops[eng].append(run)

    def dma(self, qeng, dst, src, need=(), tag=None):
        sem = {"sync": self.dQ, "scalar": self.dS, "gpsimd": self.dG}[qeng]
        self.op(qeng, lambda e, d=dst, s=src: e.dma_start(out=d, in_=s),
                need=need, tag=tag, inc=sem, inc_n=16)

    def mark(self, tag, sem):
        self.last[tag] = (sem, sem.n)

    def ts(self, out, in_, s1, s2, op0, op1=None, need=(), tag=None,
           inc=None, dr=False):
        if op1 is None:
            self.op("vector", lambda e: e.tensor_scalar(out, in_, s1, s2, op0),
                    need=need, tag=tag, inc=inc, drain=dr)
        else:
            self.op("vector",
                    lambda e: e.tensor_scalar(out, in_, s1, s2, op0, op1=op1),
                    need=need, tag=tag, inc=inc, drain=dr)

    def tt(self, out, a, b, o, need=(), tag=None, inc=None, dr=False):
        self.op("vector", lambda e: e.tensor_tensor(out, a, b, o),
                need=need, tag=tag, inc=inc, drain=dr)

    def cop(self, eng, emit, need=(), tag=None, inc=None):
        self.op(eng, emit, need=need, tag=tag, inc=inc, drain=True)

    def _sb(self, name, shape, dt):
        t = self.nc.sbuf_tensor(name, shape, dt)
        x = t.__enter__()
        self._ctx.append((name, t))
        return x

    def _pop_to(self, name):
        while self._ctx:
            n, t = self._ctx.pop()
            t.__exit__(None, None, None)
            if n == name:
                return
        raise KeyError(name)

    def alloc(self):
        nc = self.nc

        def pst(name, shape, dt):
            t = nc.psum_tensor(name, shape, dt)
            x = t.__enter__()
            self._psctx.append(t)
            return x

        def sem(name):
            t = nc.semaphore(name)
            x = t.__enter__()
            self._psctx.append(t)
            return Sem(x, name)

        self._psctx = []
        d = nc.dram_tensor
        self.x_in = d("x_in", [NTOK, C], F32, kind="ExternalInput").ap()
        self.wqg_in = d("wqg_in", [64, 128, C], F32, kind="ExternalInput").ap()
        self.wp_in = d("wp_in", [16, 128, C], F32, kind="ExternalInput").ap()
        self.w1t_in = d("w1t_in", [C, HID], F32, kind="ExternalInput").ap()
        self.w2t_in = d("w2t_in", [HID, C], F32, kind="ExternalInput").ap()
        self.bqg_in = d("bqg_in", [128, 64], F32, kind="ExternalInput").ap()
        self.bp_in = d("bp_in", [1, C], F32, kind="ExternalInput").ap()
        self.nw_in = d("nw_in", [1, C], F32, kind="ExternalInput").ap()
        self.gam_in = d("gam_in", [1, C], F32, kind="ExternalInput").ap()
        self.eye_in = d("eye_in", [128, 128], BF16, kind="ExternalInput").ap()
        self.mask_in = d("mask_in", [128, 128], BF16,
                         kind="ExternalInput").ap()
        self.out = d("out", [NTOK, C], F32, kind="ExternalOutput").ap()
        self.y_scr = d("y_scr", [NTOK, C], F32, kind="Internal").ap()
        self.xq3_scr = d("xq3_scr", [16 * 128, NTOK], BF16, kind="Internal").ap()
        self.w1q_scr = d("w1q_scr", [16 * C, 512], FP8, kind="Internal").ap()
        self.w2q_scr = d("w2q_scr", [HID, C], FP8, kind="Internal").ap()
        if self.dbg:
            self.dbg_out = d("dbg_out", list(self.dbg_shape()), F32,
                             kind="ExternalOutput").ap()

        self.id16 = self._sb("id16", [128, 128], BF16)
        self.id32 = self._sb("id32", [128, 128], F32)
        self.bqg = self._sb("bqg", [128, 64], F32)
        self.colA = self._sb("colA", [128, 64], F32)
        self.colP = self._sb("colP", [128, 64], F32)
        self.cst = self._sb("cst", [128, 16], F32)
        self.onesr = self._sb("onesr", [1, 128], F32)
        self.outy = self._sb("outy", [128, C], F32)
        self.qk = self._sb("qk", [128, 32 * 1024], BF16)
        self.vv = self._sb("vv", [128, 16 * 1024], BF16)
        self.wstg = self._sb("wstg", [128, 2 * C], F32)
        self.UB = self._sb("UB", [128, 5376], BF16)
        self.wsgn = self.UB[:, 0:2 * C]
        self.s1b = self.UB[:, 4096:5120]
        self.ktm = self._sb("ktm", [128, 384], BF16)
        self.kv = self._sb("kv", [128, C], F32)
        self.decay = self._sb("decay", [128, 128], F32)
        self.xq1 = self._sb("xq1", [128, 16 * NTOK], BF16)

        self.pb = [pst(f"pb{i}", [128, 512], F32) for i in range(4)]
        self.pa = [pst(f"pa{i}", [128, 512], F32) for i in range(3)]
        self.ptr = pst("ptr", [128, 1024], BF16)

        self.dQ = sem("dQ")
        self.dS = sem("dS")
        self.dG = sem("dG")
        self.vP = sem("vP")
        self.sP = sem("sP")
        self.tP = sem("tP")
        self.gP = sem("gP")

    def alloc_p6(self):
        self.hbuf = self._sb("hbuf", [128, HID], F32)
        self.hrun = self._sb("hrun", [128, HID // 2], F32)
        self.hb16 = self._sb("hb16", [128, HID // 2], BF16)
        self.hqtm = self._sb("hqtm", [128, HID], BF16)
        self.hqfm = self._sb("hqfm", [128, HID], BF16)
        self.w1qs = self._sb("w1qs", [128, 2 * HID], FP8)
        self.w2qs = self._sb("w2qs", [128, 2 * 1024], FP8)
        self.xq3g = self._sb("xq3g", [128, 16 * 128], BF16)
        self.ybuf = self._sb("ybuf", [128, C], F32)

    def dbg_shape(self):
        return {
            "xq1": [128, 16 * NTOK],
            "qkv": [128, 48 * 1024],
            "gdecay": [128, 128],
            "mix0": [128, C],
            "y0": [128, C],
            "yall": [NTOK, C],
            "xq2": [128, 16 * 1024],
            "xq3": [128, 16 * 128],
            "wm": [128, 8],
            "h0": [128, HID],
            "hq0": [128, HID],
        }[self.dbg]

    def col(self, i):
        return self.colA[:, i:i + 1]

    def pcol(self, i):
        return self.colP[:, i:i + 1]

    def rsqrt_col(self, dst, v, eps, sc0, sc1, iters=8, need=()):
        u, s = sc0, sc1
        self.ts(u, v, float(eps), None, OP.add, need=need, dr=True)
        self.ts(s, u, 1.0, 0.5, OP.add, OP.mult, dr=True)
        for _ in range(iters):
            self.cop("vector", lambda e: e.reciprocal(dst, s))
            self.tt(dst, u, dst, OP.mult, dr=True)
            self.tt(s, s, dst, OP.add, dr=True)
            self.ts(s, s, 0.5, None, OP.mult, dr=True)
        self.cop("vector", lambda e: e.reciprocal(dst, s))

    def alias(self, dst_tag, src_tag):
        if src_tag in self.last:
            self.last[dst_tag] = self.last[src_tag]

    def cascade(self, src, acol, u, b, xq_out, extra_add_col=None,
                need_src=(), pfx="", need_out=(), bias_col=None, width=C):
        S, V = "scalar", "vector"
        c1275 = bias_col if bias_col is not None else self.cst[:, 0:1]
        w2 = width // 2
        hs = []
        for hlf, (a0, a1) in enumerate(((0, w2), (w2, width))):
            hs.append((src[:, a0:a1], u[:, a0:a1], b[:, a0:a1],
                       xq_out[:, a0:a1], pfx + f"H{hlf}"))
        for srcH, uH, bH, xqH, px in hs:
            self.op(S, lambda e, srcH=srcH, uH=uH: e.activation(
                uH, srcH, AF.Identity, bias=c1275, scale=acol),
                need=tuple(need_src) + (px + "uw",), tag=px + "u",
                inc=self.sP)
        for j in range(8):
            tb = self.cst[:, 1:2]
            for srcH, uH, bH, xqH, px in hs:
                self.op(S, lambda e, bH=bH, uH=uH, tb=tb: e.activation(
                    bH, uH, AF.Tanh, bias=tb, scale=GBIG),
                    need=(px + "u", px + "bw"), tag=px + "b", inc=self.sP)
            kk = -CAS_K[j] / 2.0
            for srcH, uH, bH, xqH, px in hs:
                self.op(V, lambda e, bH=bH, uH=uH, kk=kk:
                        e.scalar_tensor_tensor(uH, bH, kk, uH, OP.mult,
                                               OP.add),
                        need=(px + "b",), tag=px + "u", inc=self.vP)
                self.mark(px + "bw", self.vP)
        self.alias(pfx + "b", hs[1][4] + "b")
        for srcH, uH, bH, xqH, px in hs:
            if extra_add_col is None:
                self.op(V, lambda e, srcH=srcH, uH=uH, xqH=xqH:
                        e.scalar_tensor_tensor(xqH, srcH, acol, uH, OP.mult,
                                               OP.subtract),
                        need=(px + "u",) + tuple(need_src) + tuple(need_out),
                        tag=pfx + "xq", inc=self.vP)
            else:
                self.op(V, lambda e, srcH=srcH, uH=uH: e.scalar_tensor_tensor(
                    uH, srcH, acol, uH, OP.mult, OP.subtract),
                    need=(px + "u",) + tuple(need_src), tag=px + "t1",
                    inc=self.vP)
                self.ts(xqH, uH, extra_add_col, None, OP.add,
                        need=(px + "t1",) + tuple(need_out), tag=pfx + "xq",
                        inc=self.vP)
            self.mark(px + "uw", self.vP)
        self.mark(pfx + "uw", self.vP)

    def bcast_row_dram(self, dst_tile, row_in, n, need=(), tag=None):
        T, V = "tensor", "vector"
        self.dma("sync", dst_tile[0:1, 0:n], row_in[0:1, 0:n],
                 need=tuple(need), tag=(tag or "bc") + "d")
        for i in range((n + 511) // 512):
            w = min(512, n - i * 512)
            bk = i % 2
            pbk = self.pb[bk]
            rhs = dst_tile[0:1, i * 512:i * 512 + w]
            self.op(T, lambda e, w=w, pbk=pbk, rhs=rhs: e.matmul(
                pbk[:, 0:w], self.onesr[0:1, :], rhs, start=True, stop=True),
                need=((tag or "bc") + "d", "onesr", f"pbfree{bk}"),
                tag=f"pbmm{bk}", inc=self.tP)
            self.op(V, lambda e, i=i, w=w, pbk=pbk: e.tensor_copy(
                dst_tile[:, i * 512:i * 512 + w], pbk[:, 0:w]),
                need=(f"pbmm{bk}",), tag=tag, inc=self.vP)
            self.mark(f"pbfree{bk}", self.vP)

    def colsum_bcast(self, dst_col, src_col, scale, need=(), tag=None):
        T, V = "tensor", "vector"
        onec = self.colA[:, 63:64]
        r11 = self.colA[:, 62:63]
        self.op(T, lambda e: e.matmul(self.pa[2][0:1, 0:1], onec, src_col,
                                      start=True, stop=True),
                need=tuple(need) + ("onec", "pa2free"), tag="pa2mm",
                inc=self.tP)
        self.op(V, lambda e: e.tensor_copy(r11[0:1, 0:1],
                                           self.pa[2][0:1, 0:1]),
                need=("pa2mm",), tag="r11", inc=self.vP, drain=True)
        self.mark("pa2free", self.vP)
        self.op(T, lambda e: e.matmul(self.pa[2][:, 0:1],
                                      self.onesr[0:1, :], r11[0:1, 0:1],
                                      start=True, stop=True),
                need=("r11", "onesr", "pa2free"), tag="pa2mm", inc=self.tP)
        self.op(V, lambda e, scale=scale: e.tensor_scalar(
            dst_col, self.pa[2][:, 0:1], float(scale), None, OP.mult),
            need=("pa2mm",), tag=tag, inc=self.vP, drain=True)
        self.mark("pa2free", self.vP)

    def plan(self):
        self.p0_consts()
        self.p1_means()
        for tc in range(16):
            self.p2_chunk(tc)
        if self.dbg == "xq1":
            self.dma("gpsimd", self.dbg_out[:], self.xq1[:], need=("xq1",),
                     tag="dbg")
        for hf in range(2):
            self.p3_qkv(hf)
            if self.dbg == "qkv" and hf == 0:
                self.dma("gpsimd", self.dbg_out[:, 0:32 * 1024], self.qk[:],
                         need=("qkv",), tag="dbg")
                self.dma("gpsimd", self.dbg_out[:, 32 * 1024:48 * 1024],
                         self.vv[:], need=("qkv",), tag="dbg")
                self.mark("qkv", self.dG)
                self.mark("vvfree", self.dG)
            if self.dbg == "gdecay" and hf == 0:
                self.dma("sync", self.dbg_out[:], self.decay[:], need=("dec",),
                         tag="dbg")
                self.mark("dec", self.dQ)
            for i in range(8):
                self.p4_attn(hf, i)
            self.mark("qkvfree", self.tP)
            if self.dbg == "xq2" and hf == 0:
                self.dma("gpsimd", self.dbg_out[:], self.vv[:], need=("xq2",),
                         tag="dbg")
                self.mark("xq2", self.dG)
            self.p5_setup(hf)
            for i in range(8):
                self.p5_chunk(hf, i)
            self.mark("wpqr", self.tP)
        self.mark("zfree_v", self.vP)
        self.mark("zfree_s", self.sP)
        self.mark("zfree_t", self.tP)
        self.mark("zfree_q", self.dQ)
        self._pop_to("qk")
        self.alloc_p6()
        self.p5x_wquant()
        for g in range(16):
            self.p6_group(g)
        self.fin()

    def p0_consts(self):
        V, Q = "vector", "sync"
        self.dma(Q, self.id16[:], self.eye_in[:], tag="id16")
        self.dma(Q, self.UB[:, UB_MASK:UB_MASK + 128], self.mask_in[:],
                 tag="mask")
        self.dma(Q, self.bqg[:], self.bqg_in[:], tag="bqg")
        self.op(V, lambda e: e.tensor_copy(self.id32[:], self.id16[:]),
                need=("id16",), tag="id32", inc=self.vP, drain=True)
        self.op(V, lambda e: e.memset(self.kv[:], 0.0), tag="kv", inc=self.vP)
        self.op(V, lambda e: e.memset(self.onesr[:, :], 1.0), tag="onesr",
                inc=self.vP, drain=True)
        self.op(V, lambda e: e.memset(self.colA[:, 63:64], 1.0), tag="onec",
                inc=self.vP, drain=True)
        self.op(V, lambda e: e.memset(self.cst[:, 0:1], 0.5), tag="cst",
                inc=self.vP, drain=True)
        self.op(V, lambda e: e.memset(self.cst[:, 1:2], 64.0), tag="cst",
                inc=self.vP, drain=True)
        self.op(V, lambda e: e.memset(self.cst[:, 9:10], -0.5), tag="cst",
                inc=self.vP, drain=True)
        self.op(V, lambda e: e.memset(self.cst[:, 10:11], 0.5), tag="cst",
                inc=self.vP, drain=True)

    def p1_means(self):
        V, Q = "vector", "sync"
        accq, accg, acc1, acc2, tmp = (self.col(0), self.col(1), self.col(2),
                                       self.col(3), self.col(4))
        for c in (accq, accg, acc1, acc2):
            self.op(V, lambda e, c=c: e.memset(c, 0.0), tag="macc",
                    inc=self.vP, drain=True)

        def meanstream(src_ap_fn, nblk, acc, pfx):
            for i in range(nblk):
                slot = i % 2
                stg = self.wstg[:, slot * C:(slot + 1) * C]
                self.dma(Q, stg, src_ap_fn(i), need=(f"{pfx}r{slot}",),
                         tag=f"{pfx}d{slot}")
                self.op(V, lambda e, stg=stg: e.tensor_reduce(
                    tmp, stg, AX.X, OP.add, apply_absolute_value=True),
                    need=(f"{pfx}d{slot}",), tag="mtmp", inc=self.vP,
                    drain=True)
                self.mark(f"{pfx}r{slot}", self.vP)
                self.tt(acc, acc, tmp, OP.add, need=("mtmp", "macc"),
                        tag="macc", inc=self.vP, dr=True)

        meanstream(lambda i: self.wqg_in[i], 48, accq, "mq")
        meanstream(lambda i: self.wqg_in[48 + i], 16, accg, "mg")
        meanstream(lambda i: self.w1t_in[(i // 4) * 128:(i // 4) * 128 + 128,
                                         (i % 4) * C:(i % 4) * C + C],
                   64, acc1, "m1")
        meanstream(lambda i: self.w2t_in[i * 128:(i + 1) * 128, :],
                   64, acc2, "m2")
        self.colsum_bcast(self.pcol(PC_MQ), accq, 1.0 / (6144.0 * 2048.0),
                          need=("macc",), tag="pcmq")
        self.colsum_bcast(self.pcol(PC_MG), accg, 1.0 / (2048.0 * 2048.0),
                          need=("macc",), tag="pcmg")
        self.colsum_bcast(self.pcol(PC_M1), acc1, 1.0 / (8192.0 * 2048.0),
                          need=("macc",), tag="pcm1")
        self.colsum_bcast(self.pcol(PC_M2), acc2, 1.0 / (8192.0 * 2048.0),
                          need=("macc",), tag="pcm2")
        for i, t in ((PC_MQ, "pcmq"), (PC_MG, "pcmg"), (PC_M1, "pcm1"),
                     (PC_M2, "pcm2")):
            self.ts(self.pcol(i), self.pcol(i), 1e-5, None, OP.max,
                    need=(t,), tag=t, inc=self.vP, dr=True)
        self.cop(V, lambda e: e.reciprocal(self.pcol(PC_IM1),
                                           self.pcol(PC_M1)),
                 need=("pcm1",), tag="pcim1", inc=self.vP)
        self.cop(V, lambda e: e.reciprocal(self.pcol(PC_IM2),
                                           self.pcol(PC_M2)),
                 need=("pcm2",), tag="pcim2", inc=self.vP)
        if self.dbg == "wm":
            self.dma(Q, self.dbg_out[:], self.colP[:, 0:8],
                     need=("pcmq", "pcmg", "pcim1", "pcim2"), tag="dbg")

    def p2_chunk(self, tc):
        S, V, T, Q = "scalar", "vector", "tensor", "sync"
        xt = self.wstg[:, 0:C]
        u = self.wstg[:, C:2 * C]
        b = self.wsgn[:, 0:C]
        xqtm = b
        (csum, cssq, cmean, cvar, cistd, cmx, cmn, cxm, cxsc, cq, ca, ccm,
         s0, s1) = [self.col(i) for i in range(8, 22)]
        self.dma(Q, xt, self.x_in[tc * 128:(tc + 1) * 128, :],
                 need=("xw", "mqr0", "m2r0"), tag="xd")
        self.op(V, lambda e: e.tensor_reduce(csum, xt, AX.X, OP.add),
                need=("xd",), tag="csum", inc=self.vP, drain=True)
        self.op(V, lambda e: e.tensor_tensor(u, xt, xt, OP.mult),
                need=("xd", "c2uw", "mqr1", "m2r1"), tag="cssqm", inc=self.vP)
        self.op(V, lambda e: e.tensor_reduce(cssq, u, AX.X, OP.add),
                need=("cssqm",), tag="cssq", inc=self.vP, drain=True)
        self.op(V, lambda e: e.tensor_reduce(cmx, xt, AX.X, OP.max),
                need=("xd",), tag="cmx", inc=self.vP, drain=True)
        self.op(V, lambda e: e.tensor_reduce(cmn, xt, AX.X, OP.min),
                need=("xd",), tag="cmn", inc=self.vP, drain=True)
        self.ts(cmean, csum, 1.0 / C, None, OP.mult, dr=True)
        self.ts(cvar, cssq, 1.0 / C, None, OP.mult, need=("cssq",), dr=True)
        self.tt(s0, cmean, cmean, OP.mult, dr=True)
        self.tt(cvar, cvar, s0, OP.subtract, dr=True)
        self.rsqrt_col(cistd, cvar, 1e-5, s0, s1)
        self.tt(cmx, cmx, cmean, OP.subtract, dr=True)
        self.tt(cmn, cmean, cmn, OP.subtract, dr=True)
        self.tt(cxm, cmx, cmn, OP.max, dr=True)
        self.tt(cxm, cxm, cistd, OP.mult, dr=True)
        self.ts(cxsc, cxm, 1e-5, None, OP.max, dr=True)
        self.cop(V, lambda e: e.reciprocal(cq, cxsc))
        self.ts(cq, cq, 127.0, None, OP.mult, dr=True)
        self.tt(ca, cistd, cq, OP.mult, dr=True)
        self.tt(ccm, cmean, ca, OP.mult, dr=True)
        self.ts(ccm, ccm, -1.0, None, OP.mult, dr=True)
        cb2 = self.col(22)
        self.ts(cb2, ccm, 0.5, None, OP.add, tag="cols1", inc=self.vP,
                dr=True)
        self.op(V, lambda e, tc=tc: e.tensor_copy(
            self.colP[:, PC_XSC1 + tc:PC_XSC1 + tc + 1], cxsc),
            tag="xscsv", inc=self.vP, drain=True)
        self.mark("c2src", self.vP)
        self.cascade(xt, ca, u, b, xqtm, extra_add_col=ccm,
                     need_src=("c2src", "cols1"), pfx="c2", bias_col=cb2)
        self.mark("xw", self.vP)
        xq1v = self.xq1[:].rearrange("p (cc t) -> p cc t", cc=16)
        ptrv = self.ptr[:, 0:512].rearrange("p (k t) -> p k t", k=4)
        for grp in range(4):
            for k in range(4):
                cc = grp * 4 + k
                self.op(T, lambda e, cc=cc, k=k: e.transpose(
                    self.ptr[:, k * 128:(k + 1) * 128],
                    xqtm[:, cc * 128:(cc + 1) * 128], self.id16[:]),
                    need=(("c2xq", "ptrfree", "id16") if k == 0 else ()),
                    tag=("ptrmm" if k == 3 else None),
                    inc=(self.tP if k == 3 else None))
            self.op(V, lambda e, grp=grp, tc=tc: e.tensor_copy(
                xq1v[:, grp * 4:(grp + 1) * 4, tc * 128:(tc + 1) * 128], ptrv),
                need=("ptrmm",), tag="xq1", inc=self.vP)
            self.mark("ptrfree", self.vP)
        self.mark("c2H0bw", self.tP)
        self.mark("c2H1bw", self.tP)
        self.mark("c2xqr", self.tP)

    def p3_qkv(self, hf):
        S, V, T, Q = "scalar", "vector", "tensor", "sync"
        for tch in range(8):
            gtc = hf * 8 + tch
            cxs = self.colP[:, PC_XSC1 + gtc:PC_XSC1 + gtc + 1]
            self.op(T, lambda e, cxs=cxs: e.transpose(
                self.pa[2][0:1, 0:128], cxs, self.id32[:]),
                need=("xscsv", "pa2free", "id32"), tag="pa2mm", inc=self.tP)
            self.op(V, lambda e, tch=tch: e.tensor_copy(
                self.outy[0:1, tch * 128:(tch + 1) * 128],
                self.pa[2][0:1, 0:128]),
                need=("pa2mm", "outyr", "m2r0", "m2r1"), tag="xsrow",
                inc=self.vP, drain=True)
            self.mark("pa2free", self.vP)
        def build_s1(mcol, needr):
            for blk in range(2):
                self.op(T, lambda e, blk=blk: e.matmul(
                    self.pb[blk][:, 0:512], self.onesr[0:1, :],
                    self.outy[0:1, blk * 512:(blk + 1) * 512],
                    start=True, stop=True),
                    need=("xsrow", "onesr", f"pbfree{blk}", "s1br"),
                    tag=f"pbmm{blk}", inc=self.tP)
                self.op(V, lambda e, blk=blk, mcol=mcol: e.tensor_scalar(
                    self.s1b[:, blk * 512:(blk + 1) * 512],
                    self.pb[blk][:, 0:512], mcol, 1.0 / 127.0,
                    OP.mult, op1=OP.mult),
                    need=(f"pbmm{blk}", needr, "s1br", "pTmm", "atr", "eex"),
                    tag="s1b", inc=self.vP)
                self.mark(f"pbfree{blk}", self.vP)
        build_s1(self.pcol(PC_MQ), "pcmq")
        self.mark("gevr", self.tP)

        gv = self.outy
        for oc in range(64):
            if oc == 48:
                self.mark("s1br", self.vP)
                build_s1(self.pcol(PC_MG), "pcmg")
            slot = oc % 2
            stg = self.wstg[:, slot * C:(slot + 1) * C]
            sgn = self.wsgn[:, slot * C:(slot + 1) * C]
            self.dma(Q, stg, self.wqg_in[oc],
                     need=(f"wsg{slot}", "xw", "c2uw", "c5uw"),
                     tag=f"wd{slot}")
            self.op(S, lambda e, stg=stg, sgn=sgn: e.activation(
                sgn, stg, AF.Sign),
                need=(f"wd{slot}", f"wsgr{slot}", "c2xq", "c2b", "c2xqr",
                      "c4xq", "c4b", "c4xqr", "c5xq", "c5b", "c5xqr",
                      "packr", "atr", "pbhmm0", "pbhmm1"),
                tag=f"wsg{slot}", inc=self.sP)
            for ts_ in range(2):
                bank = ts_ % 2
                pbk = self.pb[bank]
                t0 = hf * 1024 + ts_ * 512
                for cc in range(16):
                    first, lastm = cc == 0, cc == 15
                    src = self.xq1[:, cc * NTOK + t0:cc * NTOK + t0 + 512]
                    lhs = sgn[:, cc * 128:(cc + 1) * 128]
                    self.op(T, lambda e, lhs=lhs, src=src, pbk=pbk,
                            first=first, lastm=lastm: e.matmul(
                                pbk[:, 0:512], lhs, src,
                                start=first, stop=lastm),
                            need=((f"wsg{slot}", "xq1", f"pbfree{bank}")
                                  if first else ()),
                            tag=(f"pbmm{bank}" if lastm else None),
                            inc=(self.tP if lastm else None))
                self.mark(f"wsgr{slot}", self.tP)
                if oc < 48:
                    if oc < 32:
                        dst = self.qk[:, oc * 1024 + ts_ * 512:
                                      oc * 1024 + ts_ * 512 + 512]
                        ndst = ("wpqr",)
                    else:
                        dst = self.vv[:, (oc - 32) * 1024 + ts_ * 512:
                                      (oc - 32) * 1024 + ts_ * 512 + 512]
                        ndst = ("vvfree",)
                    sb_ = self.s1b[:, ts_ * 512:(ts_ + 1) * 512]
                    self.tt(dst, pbk[:, 0:512], sb_, OP.mult,
                            need=(f"pbmm{bank}", "s1b") + ndst, tag="qkvm",
                            inc=self.vP)
                    self.ts(dst, dst, self.bqg[:, oc:oc + 1], None, OP.add,
                            need=("bqg",), tag="qkv", inc=self.vP)
                    self.mark(f"pbfree{bank}", self.vP)
                else:
                    sb_ = self.s1b[:, ts_ * 512:(ts_ + 1) * 512]
                    gdst = gv[:, ts_ * 512:(ts_ + 1) * 512]
                    self.tt(gdst, pbk[:, 0:512], sb_, OP.mult,
                            need=(f"pbmm{bank}", "s1bg", "gevr"), tag="gmul",
                            inc=self.vP, dr=True)
                    self.mark(f"pbfree{bank}", self.vP)
                    self.op(S, lambda e, gdst=gdst, oc=oc: e.activation(
                        gdst, gdst, AF.Sigmoid, bias=self.bqg[:, oc:oc + 1]),
                        need=("gmul", "bqg"), tag="gsig", inc=self.sP,
                        drain=True)
            if oc == 48:
                pass
            if oc >= 48:
                h = oc - 48
                gvv = gv[:, 0:1024].rearrange("p (s q) -> p s q", s=8)
                self.op(V, lambda e, h=h, gvv=gvv: e.tensor_reduce(
                    self.decay[:, h * 8:h * 8 + 8], gvv, AX.X, OP.add),
                    need=("gsig",), tag="dec0", inc=self.vP, drain=True)
                self.op(V, lambda e, h=h: e.tensor_scalar(
                    self.decay[:, h * 8:h * 8 + 8],
                    self.decay[:, h * 8:h * 8 + 8], 1.0 / 128.0, None,
                    OP.mult), need=("dec0",), tag="dec", inc=self.vP,
                    drain=True)
                self.mark("gevr", self.vP)
        self.mark("outyr", self.vP)

    def p4_attn(self, hf, i):
        S, V, T = "scalar", "vector", "tensor"
        cR, cRb = self.col(24), self.col(25)
        rs = self.colA[:, 37:41]
        qp = self.UB[:, UB_QP:UB_QP + 512]
        kp = self.UB[:, UB_KP:UB_KP + 512]
        vp = self.UB[:, UB_VP:UB_VP + 512]
        vt = self.UB[:, UB_VT:UB_VT + 512]
        lb = self.UB[:, UB_LB:UB_LB + 2048]
        eex = self.UB[:, UB_EEX:UB_EEX + 512]
        at = self.UB[:, UB_AT:UB_AT + 512]
        mask8 = self.UB[:, UB_MASK:UB_MASK + 128]
        qhv = self.qk[:, 0:16 * 1024].rearrange("p (h t) -> p h t", h=16)
        khv = self.qk[:, 16 * 1024:32 * 1024].rearrange(
            "p (h t) -> p h t", h=16)
        vhv = self.vv[:].rearrange("p (h t) -> p h t", h=16)
        pdst = lambda t_: t_.rearrange("p (tb h l) -> p tb h l", tb=4, h=16)
        lbhv = lb.rearrange("p (h t) -> p h t", h=16)
        pb3v = self.pb[3][:].rearrange("p (tb h l) -> p tb h l", tb=4, h=16)
        for g in range(4):
            off = i * 128 + g * 32
            for nm, dst, srcv in (("q", qp, qhv), ("k", kp, khv),
                                  ("v", vp, vhv)):
                sv = srcv[:, :, off:off + 32].rearrange(
                    "p h (tb l) -> p tb h l", l=8)
                self.op(V, lambda e, dst=dst, sv=sv: e.tensor_copy(
                    pdst(dst), sv),
                    need=(("qkv", "packr", "c2b", "c2xqr", "c4b", "c4xqr",
                           "c5b", "c5xqr") if nm == "q" else ()),
                    tag=("pack" if nm == "v" else None),
                    inc=(self.vP if nm == "v" else None), drain=(nm == "v"))
            for b4 in range(4):
                first, last = b4 == 0, b4 == 3
                self.op(T, lambda e, b4=b4: e.matmul(
                    self.pb[0][:, b4 * 128:(b4 + 1) * 128],
                    qp[:, b4 * 128:(b4 + 1) * 128],
                    kp[:, b4 * 128:(b4 + 1) * 128], start=True, stop=True),
                    need=(("pack", "pbfree0") if first else ()),
                    tag=("Emm" if last else None),
                    inc=(self.tP if last else None))
            self.op(V, lambda e: e.tensor_reduce(
                cR, self.pb[0][:, 0:512], AX.X, OP.max),
                need=("Emm",), tag="cRr", inc=self.vP, drain=True)
            self.ts(cRb, cR, -SC_ATT, None, OP.mult, need=("cRr",),
                    tag="cRb", inc=self.vP, dr=True)
            self.op(S, lambda e: e.activation(
                eex, self.pb[0][:, 0:512], AF.Exp, bias=cRb, scale=SC_ATT),
                need=("cRb", "eexr", "qkvm", "gmul"), tag="eex", inc=self.sP,
                drain=True)
            self.mark("pbfree0", self.sP)
            for b4 in range(4):
                self.tt(eex[:, b4 * 128:(b4 + 1) * 128],
                        eex[:, b4 * 128:(b4 + 1) * 128], mask8, OP.mult,
                        need=(("eex", "mask") if b4 == 0 else ()),
                        dr=(b4 == 3))
            self.op(V, lambda e: e.tensor_reduce(
                rs, eex.rearrange("p (b n) -> p b n", b=4), AX.X, OP.add),
                drain=True)
            self.cop(V, lambda e: e.reciprocal(rs, rs))
            for b4 in range(4):
                self.ts(eex[:, b4 * 128:(b4 + 1) * 128],
                        eex[:, b4 * 128:(b4 + 1) * 128],
                        rs[:, b4:b4 + 1], None, OP.mult,
                        tag=("anrm" if b4 == 3 else None),
                        inc=(self.vP if b4 == 3 else None), dr=(b4 == 3))
            for b4 in range(4):
                first, last = b4 == 0, b4 == 3
                self.op(T, lambda e, b4=b4: e.transpose(
                    self.ptr[:, b4 * 128:(b4 + 1) * 128],
                    eex[:, b4 * 128:(b4 + 1) * 128], self.id16[:]),
                    need=(("anrm", "ptrfree", "id16") if first else ()),
                    tag=("pTmm" if last else None),
                    inc=(self.tP if last else None))
            self.mark("eexr", self.tP)
            self.op(V, lambda e: e.tensor_copy(at, self.ptr[:, 0:512]),
                    need=("pTmm", "atr", "qkvm", "gmul"), tag="atc",
                    inc=self.vP, drain=True)
            for b4 in range(4):
                first, last = b4 == 0, b4 == 3
                self.op(T, lambda e, b4=b4: e.transpose(
                    self.ptr[:, 512 + b4 * 128:512 + (b4 + 1) * 128],
                    vp[:, b4 * 128:(b4 + 1) * 128], self.id16[:]),
                    need=(("atc",) if first else ()),
                    tag=("pVmm" if last else None),
                    inc=(self.tP if last else None))
            self.mark("packr", self.tP)
            self.op(V, lambda e: e.tensor_copy(vt, self.ptr[:, 512:1024]),
                    need=("pVmm",), tag="vtc", inc=self.vP, drain=True)
            self.mark("ptrfree", self.vP)
            for b4 in range(4):
                first, last = b4 == 0, b4 == 3
                self.op(T, lambda e, b4=b4: e.matmul(
                    self.pb[3][:, b4 * 128:(b4 + 1) * 128],
                    vt[:, b4 * 128:(b4 + 1) * 128],
                    at[:, b4 * 128:(b4 + 1) * 128], start=True, stop=True),
                    need=(("atc", "vtc", "pbfree3") if first else ()),
                    tag=("pLmm" if last else None),
                    inc=(self.tP if last else None))
            self.mark("atr", self.tP)
            lbg = lbhv[:, :, g * 32:g * 32 + 32].rearrange(
                "p h (tb l) -> p tb h l", l=8)
            self.op(V, lambda e, lbg=lbg: e.tensor_copy(lbg, pb3v),
                    need=("pLmm", "lbr", "wsgr1"), tag="lbw", inc=self.vP,
                    drain=True)
            self.mark("pbfree3", self.vP)
        kv16h = self.ktm[:, KT_KV0:KT_KV0 + 128]
        for h in range(H):
            r, j = h // 4, h % 4
            bank = self.pb[1 + (r % 2)]
            qf = self.qk[:, h * 1024 + i * 128:h * 1024 + (i + 1) * 128]
            kf = self.qk[:, (16 + h) * 1024 + i * 128:
                         (16 + h) * 1024 + (i + 1) * 128]
            vf = self.vv[:, h * 1024 + i * 128:h * 1024 + (i + 1) * 128]
            kvslice = self.kv[:, h * 128:(h + 1) * 128]
            self.op(V, lambda e, kvslice=kvslice: e.tensor_copy(
                kv16h, kvslice),
                need=("kv", "kvsR"), tag="kvs", inc=self.vP, drain=True)
            self.op(T, lambda e, bank=bank, j=j, h=h: e.matmul(
                bank[:, j * 128:(j + 1) * 128],
                lb[:, h * 128:(h + 1) * 128], self.id16[:],
                start=True, stop=False),
                need=(("lbw", f"pbfree{1 + (r % 2)}", "id16")
                      if j == 0 else ()))
            self.op(T, lambda e, bank=bank, j=j, qf=qf: e.matmul(
                bank[:, j * 128:(j + 1) * 128], qf, kv16h,
                start=False, stop=True),
                need=("kvs",), tag=f"pbhmm{r % 2}", inc=self.tP)
            self.mark("kvsR", self.tP)
            self.op(T, lambda e, kf=kf: e.transpose(
                self.ptr[:, 0:128], kf, self.id16[:]),
                need=("qkv", "ptrfree", "id16"))
            self.op(T, lambda e, vf=vf: e.transpose(
                self.ptr[:, 128:256], vf, self.id16[:]),
                tag="ptrmm", inc=self.tP)
            self.op(V, lambda e: e.tensor_copy(self.ktm[:, 0:256],
                                               self.ptr[:, 0:256]),
                    need=("ptrmm", "ktmr"), tag="ktm", inc=self.vP,
                    drain=True)
            self.mark("ptrfree", self.vP)
            self.op(T, lambda e: e.matmul(
                self.pa[2][:, 0:128], self.ktm[:, 0:128],
                self.ktm[:, 128:256], start=True, stop=True),
                need=("ktm", "pa2free"), tag="pa2mm", inc=self.tP)
            self.mark("ktmr", self.tP)
            dcol = self.decay[:, h * 8 + i:h * 8 + i + 1]
            self.op(V, lambda e, kvslice=kvslice, dcol=dcol:
                    e.scalar_tensor_tensor(
                        kvslice, kvslice, dcol, self.pa[2][:, 0:128],
                        OP.mult, OP.add),
                    need=("pa2mm", "dec", "kvs"), tag="kv", inc=self.vP)
            self.mark("pa2free", self.vP)
            if j == 3:
                for j2 in range(4):
                    h2 = r * 4 + j2
                    self.op(V, lambda e, h2=h2, j2=j2, bank=bank:
                            e.tensor_copy(
                                self.outy[:, h2 * 128:(h2 + 1) * 128],
                                bank[:, j2 * 128:(j2 + 1) * 128]),
                            need=((f"pbhmm{r % 2}", "outyr")
                                  if j2 == 0 else ()),
                            tag=("outcp" if j2 == 3 else None),
                            inc=(self.vP if j2 == 3 else None))
                self.mark(f"pbfree{1 + (r % 2)}", self.vP)
                self.mark("lbr", self.vP)
        if self.dbg == "mix0" and hf == 0 and i == 0:
            self.dma("sync", self.dbg_out[:], self.outy[:], need=("outcp",),
                     tag="dbg")
            self.mark("outcp", self.dQ)
        u = self.wstg[:, C:2 * C]
        bts = self.wsgn[:, 0:C]
        xqtm = bts
        (cS2, cms, crs, cM2, cxsc, cq, ca, s0, s1) = [
            self.col(j) for j in range(28, 37)]
        sq = self.wstg[:, 0:C]
        self.op(V, lambda e, sq=sq: e.tensor_tensor(
            sq, self.outy[:], self.outy[:], OP.mult),
            need=("outcp", "xw", "xd2"), tag="cS2m", inc=self.vP)
        self.op(V, lambda e, sq=sq: e.tensor_reduce(cS2, sq, AX.X, OP.add),
                need=("cS2m",), tag="cS2", inc=self.vP, drain=True)
        self.bcast_row_dram(u, self.nw_in, C,
                            need=("c4uw", "c2uw", "wd0", "wd1", "wsg0",
                                  "wsg1"), tag="nwu")
        self.tt(self.outy[:], self.outy[:], u, OP.mult,
                need=("outcp", "nwu", "cS2m"), tag="outn", inc=self.vP)
        self.mark("c4uw", self.vP)
        self.ts(cms, cS2, 1.0 / C, None, OP.mult, need=("cS2",), dr=True)
        self.rsqrt_col(crs, cms, 1e-6, s0, s1)
        self.op(V, lambda e: e.tensor_reduce(
            cM2, self.outy[:], AX.X, OP.max, apply_absolute_value=True),
            need=("outn",), tag="cM2v", inc=self.vP, drain=True)
        self.tt(cM2, cM2, crs, OP.mult, dr=True)
        self.ts(cxsc, cM2, 1e-5, None, OP.max, dr=True)
        self.cop(V, lambda e: e.reciprocal(cq, cxsc))
        self.ts(cq, cq, 127.0, None, OP.mult, dr=True)
        self.tt(ca, crs, cq, OP.mult, tag="cols2", inc=self.vP, dr=True)
        gi = hf * 8 + i
        self.op(V, lambda e, gi=gi: e.tensor_copy(
            self.colP[:, PC_XSC2 + gi:PC_XSC2 + gi + 1], cxsc),
            tag="xsc2sv", inc=self.vP, drain=True)
        self.mark("c4src", self.vP)
        self.cascade(self.outy[:], ca, u, bts, xqtm,
                     need_src=("c4src", "cols2"), pfx="c4")
        self.mark("outyr", self.vP)
        xq2v = self.vv[:].rearrange("p (cc t) -> p cc t", cc=16)
        ptrv = self.ptr[:, 0:512].rearrange("p (k t) -> p k t", k=4)
        for grp in range(4):
            for k in range(4):
                cc = grp * 4 + k
                self.op(T, lambda e, cc=cc, k=k: e.transpose(
                    self.ptr[:, k * 128:(k + 1) * 128],
                    xqtm[:, cc * 128:(cc + 1) * 128], self.id16[:]),
                    need=(("c4xq", "ptrfree") if k == 0 else ()),
                    tag=("ptrmm" if k == 3 else None),
                    inc=(self.tP if k == 3 else None))
            self.op(V, lambda e, grp=grp, i=i: e.tensor_copy(
                xq2v[:, grp * 4:(grp + 1) * 4, i * 128:(i + 1) * 128], ptrv),
                need=("ptrmm", "pbhmm0", "pbhmm1", "vvfree"), tag="xq2",
                inc=self.vP)
            self.mark("ptrfree", self.vP)
        self.mark("c4H0bw", self.tP)
        self.mark("c4H1bw", self.tP)
        self.mark("c4xqr", self.tP)

    def p5_setup(self, hf):
        S, V, Q = "scalar", "vector", "sync"
        accp, tmp = self.col(40), self.col(41)
        self.op(V, lambda e: e.memset(accp, 0.0), tag="pacc", inc=self.vP,
                drain=True)
        for pc in range(16):
            slot = pc % 2
            stg = self.wstg[:, slot * C:(slot + 1) * C]
            self.dma(Q, stg, self.wp_in[pc],
                     need=(f"wpsg{slot}", "xw", "c4uw", "mpv"),
                     tag=f"wd{slot}")
            dst = self.qk[:, pc * C:(pc + 1) * C]
            self.op(S, lambda e, stg=stg, dst=dst: e.activation(
                dst, stg, AF.Sign),
                need=(f"wd{slot}", "qkvfree", "c4xq", "c4b", "c4xqr"),
                tag=f"wpsg{slot}", inc=self.sP)
            self.op(V, lambda e, stg=stg: e.tensor_reduce(
                tmp, stg, AX.X, OP.add, apply_absolute_value=True),
                need=(f"wd{slot}",), tag="ptmp", inc=self.vP, drain=True)
            self.mark("mpv", self.vP)
            self.tt(accp, accp, tmp, OP.add, need=("ptmp", "pacc"),
                    tag="pacc", inc=self.vP, dr=True)
        self.mark("wpq", self.sP)
        self.colsum_bcast(self.pcol(PC_MP), accp, 1.0 / (2048.0 * 2048.0),
                          need=("pacc",), tag="pcmp")
        self.ts(self.pcol(PC_MP), self.pcol(PC_MP), 1e-5, None, OP.max,
                need=("pcmp",), tag="pcmp", inc=self.vP, dr=True)

    def p5_chunk(self, hf, i):
        S, V, T, Q = "scalar", "vector", "tensor", "sync"
        gi = hf * 8 + i
        cs2, t0c = self.col(42), self.col(43)
        self.tt(t0c, self.colP[:, PC_XSC2 + gi:PC_XSC2 + gi + 1],
                self.pcol(PC_MP), OP.mult, need=("xsc2sv", "pcmp"), dr=True)
        self.ts(cs2, t0c, 1.0 / 127.0, None, OP.mult, tag="cs2", inc=self.vP,
                dr=True)
        for os_ in range(4):
            bank = os_ % 2
            pbk = self.pb[bank]
            for cc in range(16):
                first, lastm = cc == 0, cc == 15
                lhs = self.vv[:, cc * 1024 + i * 128:cc * 1024 + (i + 1) * 128]
                rhs = self.qk[:, cc * C + os_ * 512:cc * C + os_ * 512 + 512]
                self.op(T, lambda e, lhs=lhs, rhs=rhs, pbk=pbk, first=first,
                        lastm=lastm: e.matmul(pbk[:, 0:512], lhs, rhs,
                                              start=first, stop=lastm),
                        need=((f"pbfree{bank}", "wpq", "xq2") if first
                              else ()),
                        tag=(f"pbmm{bank}" if lastm else None),
                        inc=(self.tP if lastm else None))
            self.op(S, lambda e, os_=os_, pbk=pbk: e.activation(
                self.outy[:, os_ * 512:(os_ + 1) * 512], pbk[:, 0:512],
                AF.Identity, scale=cs2),
                need=(f"pbmm{bank}", "cs2", "outyr"), tag="proj", inc=self.sP,
                drain=True)
            self.mark(f"pbfree{bank}", self.sP)
        u = self.wstg[:, C:2 * C]
        self.bcast_row_dram(u, self.bp_in, C,
                            need=("c5uw", "c4uw", "c2uw", "wd0", "wd1",
                                  "wsg0", "wsg1", "wpsg0", "wpsg1"),
                            tag="bpu")
        self.tt(self.outy[:], self.outy[:], u, OP.add, need=("proj", "bpu"),
                tag="yb", inc=self.vP)
        self.mark("c5uw", self.vP)
        xt = self.wstg[:, 0:C]
        self.dma(Q, xt, self.x_in[gi * 128:(gi + 1) * 128, :],
                 need=("xw", "wd0", "wd1", "cS2"), tag="xd2")
        self.tt(self.outy[:], self.outy[:], xt, OP.add,
                need=("yb", "xd2"), tag="yrow", inc=self.vP)
        self.dma(Q, self.y_scr[gi * 128:(gi + 1) * 128, :], self.outy[:],
                 need=("yrow",), tag="yst")
        self.mark("outyr", self.dQ)
        if self.dbg == "y0" and gi == 0:
            self.dma(Q, self.dbg_out[:], self.outy[:], need=("yrow",),
                     tag="dbg")
            self.mark("yrow", self.dQ)
        bts = self.wsgn[:, 0:C]
        xqtm = bts
        (csum, cssq, cmean, cvar, cistd, cw, crs2, czs, czb, cM3, cxsc, cq,
         ca, s0, s1) = [self.col(j) for j in range(44, 59)]
        self.op(V, lambda e: e.tensor_reduce(csum, self.outy[:], AX.X,
                                             OP.add),
                need=("yrow",), tag="c5sum", inc=self.vP, drain=True)
        self.op(V, lambda e: e.tensor_tensor(u, self.outy[:], self.outy[:],
                                             OP.mult),
                need=("yrow", "c5uw"), tag="c5ssqm", inc=self.vP)
        self.op(V, lambda e: e.tensor_reduce(cssq, u, AX.X, OP.add),
                need=("c5ssqm",), tag="c5ssq", inc=self.vP, drain=True)
        self.ts(cmean, csum, 1.0 / C, None, OP.mult, need=("c5sum",), dr=True)
        self.ts(cvar, cssq, 1.0 / C, None, OP.mult, need=("c5ssq",), dr=True)
        self.tt(s0, cmean, cmean, OP.mult, dr=True)
        self.tt(cvar, cvar, s0, OP.subtract, dr=True)
        self.rsqrt_col(cistd, cvar, 1e-5, s0, s1)
        self.tt(cw, cvar, cistd, OP.mult, dr=True)
        self.tt(cw, cw, cistd, OP.mult, dr=True)
        self.rsqrt_col(crs2, cw, 1e-6, s0, s1)
        self.tt(czs, cistd, crs2, OP.mult, dr=True)
        self.tt(czb, cmean, czs, OP.mult, dr=True)
        self.ts(czb, czb, -1.0, None, OP.mult, tag="c5zb", inc=self.vP,
                dr=True)
        self.op(S, lambda e: e.activation(xt, self.outy[:], AF.Identity,
                                          bias=czb, scale=czs),
                need=("c5zb", "yrow", "xw"), tag="c5z", inc=self.sP)
        self.mark("c5uw2", self.sP)
        self.bcast_row_dram(u, self.gam_in, C,
                            need=("c5uw", "c5ssq"), tag="gamu")
        self.tt(xt, xt, u, OP.mult, need=("c5z", "gamu"), tag="c5zg",
                inc=self.vP)
        self.mark("c5uw", self.vP)
        self.op(V, lambda e: e.tensor_reduce(
            cM3, xt, AX.X, OP.max, apply_absolute_value=True),
            need=("c5zg",), tag="c5m", inc=self.vP, drain=True)
        self.ts(cxsc, cM3, 1e-5, None, OP.max, dr=True)
        self.cop(V, lambda e: e.reciprocal(cq, cxsc))
        self.ts(ca, cq, 127.0, None, OP.mult, tag="cols3", inc=self.vP,
                dr=True)
        self.op(V, lambda e, gi=gi: e.tensor_copy(
            self.colP[:, PC_XSC3 + gi:PC_XSC3 + gi + 1], cxsc),
            tag="xsc3sv", inc=self.vP, drain=True)
        self.mark("c5src", self.vP)
        self.cascade(xt, ca, u, bts, xqtm,
                     need_src=("c5src", "cols3", "c5zg"), pfx="c5")
        self.mark("xw", self.vP)
        xq3v = self.vv[:].rearrange("p (cc t) -> p cc t", cc=16)
        ptrv = self.ptr[:, 0:512].rearrange("p (k t) -> p k t", k=4)
        for grp in range(4):
            for k in range(4):
                cc = grp * 4 + k
                self.op(T, lambda e, cc=cc, k=k: e.transpose(
                    self.ptr[:, k * 128:(k + 1) * 128],
                    xqtm[:, cc * 128:(cc + 1) * 128], self.id16[:]),
                    need=(("c5xq", "ptrfree") if k == 0 else ()),
                    tag=("ptrmm" if k == 3 else None),
                    inc=(self.tP if k == 3 else None))
            self.op(V, lambda e, grp=grp, i=i: e.tensor_copy(
                xq3v[:, grp * 4:(grp + 1) * 4, i * 128:(i + 1) * 128], ptrv),
                need=("ptrmm", "pbmm0", "pbmm1", "proj"), tag="xq3f",
                inc=self.vP, drain=True)
            self.mark("ptrfree", self.vP)
        self.mark("c5H0bw", self.tP)
        self.mark("c5H1bw", self.tP)
        self.mark("c5xqr", self.tP)
        for cc in range(16):
            self.dma(Q, self.xq3_scr[cc * 128:(cc + 1) * 128,
                                     gi * 128:(gi + 1) * 128],
                     self.vv[:, cc * 1024 + i * 128:cc * 1024 + (i + 1) * 128],
                     need=("xq3f",), tag="xq3st")
        self.mark("vvfree", self.dQ)
        if self.dbg == "xq3" and gi == 0:
            dv = self.dbg_out[:].rearrange("p (cc t) -> p cc t", cc=16)
            sv = self.vv[:].rearrange("p (cc t) -> p cc t", cc=16)
            self.dma("gpsimd", dv, sv[:, :, 0:128], need=("xq3f",), tag="dbg")
            self.mark("vvfree", self.dG)

    def p5x_wquant(self):
        S, V, Q = "scalar", "vector", "sync"
        zf = ("zfree_v", "zfree_s", "zfree_t", "zfree_q")
        for rc in range(16):
            self.dma(Q, self.hbuf[:], self.w1t_in[rc * 128:(rc + 1) * 128, :],
                     need=("qh1r",) + zf, tag="qh1d")
            self.op(S, lambda e: e.activation(
                self.hqtm[:], self.hbuf[:], AF.Sign, bias=self.cst[:, 9:10],
                scale=self.pcol(PC_IM1)),
                need=("qh1d", "pcim1", "hqtmw") + zf, tag="wq1a", inc=self.sP)
            self.op(S, lambda e: e.activation(
                self.hqfm[:], self.hbuf[:], AF.Sign, bias=self.cst[:, 10:11],
                scale=self.pcol(PC_IM1)),
                need=("qh1d", "hqfmw") + zf, tag="wq1b", inc=self.sP)
            self.mark("qh1r", self.sP)
            self.op(V, lambda e: e.tensor_tensor(
                self.w1qs[:, 0:HID], self.hqtm[:], self.hqfm[:], OP.add),
                need=("wq1a", "wq1b", "w1qsr") + zf, tag="w1q8", inc=self.vP)
            self.mark("hqtmw", self.vP)
            self.mark("hqfmw", self.vP)
            for hs in range(16):
                self.dma(Q, self.w1q_scr[hs * C + rc * 128:
                                         hs * C + (rc + 1) * 128, :],
                         self.w1qs[:, hs * 512:(hs + 1) * 512],
                         need=("w1q8",), tag="w1qst")
            self.mark("w1qsr", self.dQ)
        for rc in range(64):
            self.dma(Q, self.hbuf[:, 0:C],
                     self.w2t_in[rc * 128:(rc + 1) * 128, :],
                     need=("qh2r",), tag="qh2d")
            self.op(S, lambda e: e.activation(
                self.hqtm[:, 0:C], self.hbuf[:, 0:C], AF.Sign,
                bias=self.cst[:, 9:10], scale=self.pcol(PC_IM2)),
                need=("qh2d", "pcim2", "hqtmw"), tag="wq2a", inc=self.sP)
            self.op(S, lambda e: e.activation(
                self.hqfm[:, 0:C], self.hbuf[:, 0:C], AF.Sign,
                bias=self.cst[:, 10:11], scale=self.pcol(PC_IM2)),
                need=("qh2d", "hqfmw"), tag="wq2b", inc=self.sP)
            self.mark("qh2r", self.sP)
            self.op(V, lambda e: e.tensor_tensor(
                self.w2qs[:, 0:C], self.hqtm[:, 0:C], self.hqfm[:, 0:C],
                OP.add),
                need=("wq2a", "wq2b", "w2qsr"), tag="w2q8", inc=self.vP)
            self.mark("hqtmw", self.vP)
            self.mark("hqfmw", self.vP)
            self.dma(Q, self.w2q_scr[rc * 128:(rc + 1) * 128, :],
                     self.w2qs[:, 0:C], need=("w2q8",), tag="w2qst")
            self.mark("w2qsr", self.dQ)

    def p6_group(self, g):
        S, V, T, Q = "scalar", "vector", "tensor", "sync"
        cs3, cs4, ca4, cxsc4, cq4, t0c = [self.col(j) for j in
                                          (8, 9, 10, 11, 12, 13)]
        for cc in range(16):
            self.dma("scalar", self.xq3g[:, cc * 128:(cc + 1) * 128],
                     self.xq3_scr[cc * 128:(cc + 1) * 128,
                                  g * 128:(g + 1) * 128],
                     need=("xq3gr", "xq3st", "zfree_v", "zfree_s", "zfree_t"),
                     tag="xq3gd")
        self.tt(t0c, self.colP[:, PC_XSC3 + g:PC_XSC3 + g + 1],
                self.pcol(PC_M1), OP.mult, need=("xsc3sv", "pcm1"), dr=True)
        self.ts(cs3, t0c, 1.0 / 254.0, None, OP.mult, tag="cs3", inc=self.vP,
                dr=True)
        for hs in range(16):
            slot = hs % 2
            wsl = self.w1qs[:, slot * HID:(slot + 1) * HID]
            src3 = self.w1q_scr[hs * C:(hs + 1) * C, :].rearrange(
                "(cc p) h -> p cc h", p=128)
            dst3 = wsl.rearrange("p (cc h) -> p cc h", cc=16)
            self.dma(Q, dst3, src3, need=(f"w1r{slot}", "w1qst"),
                     tag=f"w1d{slot}")
            bank = hs % 2
            pbk = self.pb[bank]
            for cc in range(16):
                first, lastm = cc == 0, cc == 15
                lhs = self.xq3g[:, cc * 128:(cc + 1) * 128]
                rhs = wsl[:, cc * 512:(cc + 1) * 512]
                self.op(T, lambda e, lhs=lhs, rhs=rhs, pbk=pbk, first=first,
                        lastm=lastm: e.matmul(pbk[:, 0:512], lhs, rhs,
                                              start=first, stop=lastm),
                        need=((f"w1d{slot}", "xq3gd", f"pbfree{bank}")
                              if first else ()),
                        tag=(f"pbmm{bank}" if lastm else None),
                        inc=(self.tP if lastm else None))
            self.mark(f"w1r{slot}", self.tP)
            self.op(S, lambda e, hs=hs, pbk=pbk: e.activation(
                self.hbuf[:, hs * 512:(hs + 1) * 512], pbk[:, 0:512],
                AF.Gelu, scale=cs3),
                need=(f"pbmm{bank}", "cs3", "hbw"), tag="hb", inc=self.sP,
                drain=True)
            self.mark(f"pbfree{bank}", self.sP)
        if self.dbg == "h0" and g == 0:
            self.dma(Q, self.dbg_out[:], self.hbuf[:], need=("hb",),
                     tag="dbg")
            self.mark("hb", self.dQ)
        self.op(V, lambda e: e.tensor_reduce(
            cxsc4, self.hbuf[:], AX.X, OP.max, apply_absolute_value=True),
            need=("hb",), tag="c6m", inc=self.vP, drain=True)
        self.ts(cxsc4, cxsc4, 1e-5, None, OP.max, dr=True)
        self.cop(V, lambda e: e.reciprocal(cq4, cxsc4))
        self.ts(ca4, cq4, 127.0, None, OP.mult, tag="cols4", inc=self.vP,
                dr=True)
        self.tt(t0c, cxsc4, self.pcol(PC_M2), OP.mult, need=("pcm2",),
                dr=True)
        self.ts(cs4, t0c, 1.0 / 254.0, None, OP.mult, tag="cs4", inc=self.vP,
                dr=True)
        for half in range(2):
            hsrc = self.hbuf[:, half * 4096:(half + 1) * 4096]
            hdst = self.hqtm[:, half * 4096:(half + 1) * 4096]
            self.cascade(hsrc, ca4, self.hrun[:], self.hb16[:], hdst,
                         need_src=("c6m", "cols4"), pfx="c6",
                         need_out=("hqtmw",), width=4096)
        self.mark("hbw", self.vP)
        for grp in range(16):
            for k in range(4):
                hc = grp * 4 + k
                self.op(T, lambda e, hc=hc, k=k: e.transpose(
                    self.ptr[:, k * 128:(k + 1) * 128],
                    self.hqtm[:, hc * 128:(hc + 1) * 128], self.id16[:]),
                    need=(("c6xq", "ptrfree") if k == 0 else ()),
                    tag=("ptrmm" if k == 3 else None),
                    inc=(self.tP if k == 3 else None))
            self.op(V, lambda e, grp=grp: e.tensor_copy(
                self.hqfm[:, grp * 512:(grp + 1) * 512], self.ptr[:, 0:512]),
                need=("ptrmm", "hqfr"), tag="hqf", inc=self.vP, drain=True)
            self.mark("ptrfree", self.vP)
        self.mark("hqtmw", self.vP)
        if self.dbg == "hq0" and g == 0:
            self.dma("gpsimd", self.dbg_out[:], self.hqfm[:], need=("hqf",),
                     tag="dbg")
            self.mark("hqf", self.dG)
        for oh in range(2):
            for hc in range(64):
                slot = hc % 2
                wsl = self.w2qs[:, slot * 1024:(slot + 1) * 1024]
                self.dma(Q, wsl,
                         self.w2q_scr[hc * 128:(hc + 1) * 128,
                                      oh * 1024:(oh + 1) * 1024],
                         need=(f"w2rd{slot}", "w2qst"), tag=f"w2d{slot}")
                for os_ in range(2):
                    first, lastm = hc == 0, hc == 63
                    pbk = self.pb[2 + os_]
                    lhs = self.hqfm[:, hc * 128:(hc + 1) * 128]
                    rhs = wsl[:, os_ * 512:(os_ + 1) * 512]
                    nd = [f"w2d{slot}"]
                    if first:
                        nd += ["hqf", f"pbfree{2 + os_}"]
                    self.op(T, lambda e, lhs=lhs, rhs=rhs, pbk=pbk,
                            first=first, lastm=lastm: e.matmul(
                                pbk[:, 0:512], lhs, rhs, start=first,
                                stop=lastm),
                            need=tuple(nd),
                            tag=(f"pbmm{2 + os_}" if lastm else None),
                            inc=self.tP)
                self.mark(f"w2rd{slot}", self.tP)
            for os_ in range(2):
                pbk = self.pb[2 + os_]
                dst = self.outy[:, oh * 1024 + os_ * 512:
                                oh * 1024 + os_ * 512 + 512]
                self.op(S, lambda e, dst=dst, pbk=pbk: e.activation(
                    dst, pbk[:, 0:512], AF.Identity, scale=cs4),
                    need=(f"pbmm{2 + os_}", "cs4", "outyr"), tag="w2e",
                    inc=self.sP, drain=True)
                self.mark(f"pbfree{2 + os_}", self.sP)
        self.mark("hqfr", self.tP)
        self.dma(Q, self.ybuf[:], self.y_scr[g * 128:(g + 1) * 128, :],
                 need=("ybr", "yst"), tag="yd")
        self.tt(self.outy[:], self.outy[:], self.ybuf[:], OP.add,
                need=("w2e", "yd"), tag="orow", inc=self.vP)
        self.mark("ybr", self.vP)
        self.dma(Q, self.out[g * 128:(g + 1) * 128, :], self.outy[:],
                 need=("orow",), tag="ost")
        self.mark("outyr", self.dQ)
        self.mark("xq3gr", self.tP)

    def fin(self):
        if self.dbg == "yall":
            self.dma("scalar", self.dbg_out[:, :], self.y_scr[:, :],
                     need=("yst",), tag="dbg")
            self.mark("dbg", self.dS)
        for s in (self.dQ, self.dS, self.dG):
            v, h = s.n, s.h
            if v:
                self.ops["sync"].append(
                    lambda e, v=v, h=h: e.wait_ge(h, v))

    def build(self):
        self.alloc()
        self.plan()
        nc = self.nc
        blk = nc.Block()
        block = blk.__enter__()

        @block.sync
        def _(e):
            for f in self.ops["sync"]:
                f(e)

        @block.scalar
        def _(e):
            for f in self.ops["scalar"]:
                f(e)

        @block.vector
        def _(e):
            for f in self.ops["vector"]:
                f(e)

        @block.tensor
        def _(e):
            for f in self.ops["tensor"]:
                f(e)

        @block.gpsimd
        def _(e):
            for f in self.ops["gpsimd"]:
                f(e)

        blk.__exit__(None, None, None)
        while self._ctx:
            self._ctx.pop()[1].__exit__(None, None, None)
        for t in reversed(self._psctx):
            t.__exit__(None, None, None)
        return nc


def stage_inputs(inputs, b):
    qkv_w = np.asarray(inputs["L0_attn_qkv_w"])
    gate_w = np.asarray(inputs["L0_attn_gate_w"])
    proj_w = np.asarray(inputs["L0_attn_proj_w"])
    w1 = np.asarray(inputs["L0_ffn_w1"])
    w2 = np.asarray(inputs["L0_ffn_w2"])

    def chunk_stationary(w):
        O = w.shape[0]
        wt = np.ascontiguousarray(w.T)
        a = wt.reshape(16, 128, O // 128, 128)
        return np.ascontiguousarray(
            a.transpose(2, 1, 0, 3).reshape(O // 128, 128, 2048))

    wqg = np.concatenate([qkv_w, gate_w], axis=0)
    bqg = np.concatenate([np.asarray(inputs["L0_attn_qkv_b"]),
                          np.asarray(inputs["L0_attn_gate_b"])])
    return {
        "x_in": np.ascontiguousarray(np.asarray(inputs["L0_attn_x"])[b]),
        "wqg_in": chunk_stationary(wqg),
        "wp_in": np.ascontiguousarray(proj_w.T.reshape(16, 128, 2048)),
        "w1t_in": np.ascontiguousarray(w1.T),
        "w2t_in": np.ascontiguousarray(w2.T),
        "bqg_in": np.ascontiguousarray(bqg.reshape(64, 128).T),
        "bp_in": np.ascontiguousarray(
            np.asarray(inputs["L0_attn_proj_b"]).reshape(1, -1)),
        "nw_in": np.ascontiguousarray(
            np.asarray(inputs["L0_attn_norm_w"]).reshape(1, -1)),
        "gam_in": np.ascontiguousarray(
            np.asarray(inputs["L0_ffn_gamma"]).reshape(1, -1)),
        "eye_in": np.eye(128).astype(ml_dtypes.bfloat16),
        "mask_in": np.ascontiguousarray(
            (np.arange(128)[:, None] % 8 == np.arange(128)[None, :] % 8)
            .astype(ml_dtypes.bfloat16)),
    }


def kernel(**inputs):
    mk = MK()
    nc = mk.build()
    in_maps = [stage_inputs(inputs, b) for b in range(8)]
    res = run_bass_kernel_spmd(nc, in_maps, core_ids=list(range(8)))
    return np.stack([r["out"] for r in res.results], axis=0)


# version 07 (final)
# speedup vs baseline: 1.2529x; 1.2529x over previous
import numpy as np
import ml_dtypes
import concourse.bass as bass
import concourse.mybir as mybir
from concourse.bass_utils import run_bass_kernel_spmd

F32 = mybir.dt.float32
BF16 = mybir.dt.bfloat16
FP8 = mybir.dt.float8e4
AF = mybir.ActivationFunctionType
OP = mybir.AluOpType
AX = mybir.AxisListType

C = 2048
NTOK = 2048
H = 16
HID = 8192
GBIG = float(2 ** 40)
SC_ATT = float(128) ** -0.5
CAS_K = [128.0, 64.0, 32.0, 16.0, 8.0, 4.0, 2.0, 1.0]
CAS_TAU = [255.0, 127.0, 63.0, 31.0, 15.0, 7.0, 3.0, 1.0]

ENGS = ("sync", "scalar", "vector", "tensor", "gpsimd")

PC_MQ, PC_MG, PC_M1, PC_M2, PC_IM1, PC_IM2, PC_MP = 0, 1, 2, 3, 4, 5, 56
PC_XSC1 = 6
PC_XSC2 = 24
PC_XSC3 = 40

KT_K, KT_V, KT_KV0 = 0, 128, 256
UB_QP, UB_KP, UB_VP, UB_VT = 0, 512, 1024, 1536
UB_B = 0
UB_LB = 2048
UB_EEX, UB_AT = 4096, 4608
UB_MASK = 5120


class Sem:
    def __init__(self, h, name):
        self.h, self.name, self.n = h, name, 0

    def inc(self, k=1):
        self.n += k
        return self.n


class MK:
    def __init__(self, dbg=None):
        self.dbg = dbg
        self.nc = bass.Bass("TRN2", target_bir_lowering=False)
        self.nc.detect_race_conditions = False
        self.ops = {e: [] for e in ENGS}
        self._ctx = []
        self.last = {}

    def op(self, eng, emit, need=(), tag=None, inc=None, inc_n=1,
           drain=False):
        waits = {}
        for t in need:
            if t in self.last:
                s, v = self.last[t]
                if waits.get(s.name, (None, -1))[1] < v:
                    waits[s.name] = (s, v)
        if inc is not None:
            val = inc.inc(inc_n)
            if tag is not None:
                self.last[tag] = (inc, val)
        wl = tuple((s.h, v) for (s, v) in waits.values())

        def run(e, emit=emit, wl=wl, inc=inc, inc_n=inc_n, drain=drain):
            for h, v in wl:
                e.wait_ge(h, v)
            if emit is None:
                return
            inst = emit(e)
            if drain:
                inst = e.drain()
            if inc is not None:
                inst.then_inc(inc.h, inc_n)
        self.ops[eng].append(run)

    def dma(self, qeng, dst, src, need=(), tag=None):
        sem = {"sync": self.dQ, "scalar": self.dS, "gpsimd": self.dG}[qeng]
        self.op(qeng, lambda e, d=dst, s=src: e.dma_start(out=d, in_=s),
                need=need, tag=tag, inc=sem, inc_n=16)

    def mark(self, tag, sem):
        self.last[tag] = (sem, sem.n)

    def ts(self, out, in_, s1, s2, op0, op1=None, need=(), tag=None,
           inc=None, dr=False):
        if op1 is None:
            self.op("vector", lambda e: e.tensor_scalar(out, in_, s1, s2, op0),
                    need=need, tag=tag, inc=inc, drain=dr)
        else:
            self.op("vector",
                    lambda e: e.tensor_scalar(out, in_, s1, s2, op0, op1=op1),
                    need=need, tag=tag, inc=inc, drain=dr)

    def tt(self, out, a, b, o, need=(), tag=None, inc=None, dr=False):
        self.op("vector", lambda e: e.tensor_tensor(out, a, b, o),
                need=need, tag=tag, inc=inc, drain=dr)

    def cop(self, eng, emit, need=(), tag=None, inc=None):
        self.op(eng, emit, need=need, tag=tag, inc=inc, drain=True)

    def _sb(self, name, shape, dt):
        t = self.nc.sbuf_tensor(name, shape, dt)
        x = t.__enter__()
        self._ctx.append((name, t))
        return x

    def _pop_to(self, name):
        while self._ctx:
            n, t = self._ctx.pop()
            t.__exit__(None, None, None)
            if n == name:
                return
        raise KeyError(name)

    def alloc(self):
        nc = self.nc

        def pst(name, shape, dt):
            t = nc.psum_tensor(name, shape, dt)
            x = t.__enter__()
            self._psctx.append(t)
            return x

        def sem(name):
            t = nc.semaphore(name)
            x = t.__enter__()
            self._psctx.append(t)
            return Sem(x, name)

        self._psctx = []
        d = nc.dram_tensor
        self.x_in = d("x_in", [NTOK, C], F32, kind="ExternalInput").ap()
        self.wqg_in = d("wqg_in", [64, 128, C], F32, kind="ExternalInput").ap()
        self.wp_in = d("wp_in", [16, 128, C], F32, kind="ExternalInput").ap()
        self.w1t_in = d("w1t_in", [C, HID], F32, kind="ExternalInput").ap()
        self.w2t_in = d("w2t_in", [HID, C], F32, kind="ExternalInput").ap()
        self.bqg_in = d("bqg_in", [128, 64], F32, kind="ExternalInput").ap()
        self.bp_in = d("bp_in", [1, C], F32, kind="ExternalInput").ap()
        self.nw_in = d("nw_in", [1, C], F32, kind="ExternalInput").ap()
        self.gam_in = d("gam_in", [1, C], F32, kind="ExternalInput").ap()
        self.eye_in = d("eye_in", [128, 128], BF16, kind="ExternalInput").ap()
        self.mask_in = d("mask_in", [128, 128], BF16,
                         kind="ExternalInput").ap()
        self.out = d("out", [NTOK, C], F32, kind="ExternalOutput").ap()
        self.y_scr = d("y_scr", [NTOK, C], F32, kind="Internal").ap()
        self.xq3_scr = d("xq3_scr", [16 * 128, NTOK], BF16, kind="Internal").ap()
        self.w1q_scr = d("w1q_scr", [16 * C, 512], FP8, kind="Internal").ap()
        self.w2q_scr = d("w2q_scr", [HID, C], FP8, kind="Internal").ap()
        if self.dbg:
            self.dbg_out = d("dbg_out", list(self.dbg_shape()), F32,
                             kind="ExternalOutput").ap()

        self.id16 = self._sb("id16", [128, 128], BF16)
        self.id32 = self._sb("id32", [128, 128], F32)
        self.bqg = self._sb("bqg", [128, 64], F32)
        self.colA = self._sb("colA", [128, 64], F32)
        self.colP = self._sb("colP", [128, 64], F32)
        self.cst = self._sb("cst", [128, 16], F32)
        self.onesr = self._sb("onesr", [1, 128], F32)
        self.outy = self._sb("outy", [128, C], F32)
        self.qk = self._sb("qk", [128, 32 * 1024], BF16)
        self.vv = self._sb("vv", [128, 16 * 1024], BF16)
        self.wstg = self._sb("wstg", [128, 2 * C], F32)
        self.UB = self._sb("UB", [128, 5376], BF16)
        self.wsgn = self.UB[:, 0:2 * C]
        self.s1b = self.UB[:, 4096:5120]
        self.ktm = self._sb("ktm", [128, 384], BF16)
        self.kv = self._sb("kv", [128, C], F32)
        self.decay = self._sb("decay", [128, 128], F32)
        self.xq1 = self._sb("xq1", [128, 16 * NTOK], BF16)

        self.pb = [pst(f"pb{i}", [128, 512], F32) for i in range(4)]
        self.pa = [pst(f"pa{i}", [128, 512], F32) for i in range(3)]
        self.ptr = pst("ptr", [128, 1024], BF16)

        self.dQ = sem("dQ")
        self.dS = sem("dS")
        self.dG = sem("dG")
        self.vP = sem("vP")
        self.sP = sem("sP")
        self.tP = sem("tP")
        self.gP = sem("gP")

    def alloc_p6(self):
        self.hbuf = self._sb("hbuf", [128, HID], F32)
        self.hrun = self._sb("hrun", [128, HID // 2], F32)
        self.hb16 = self._sb("hb16", [128, HID // 2], BF16)
        self.hqtm = self._sb("hqtm", [128, HID], BF16)
        self.hqfm = self._sb("hqfm", [128, HID], BF16)
        self.w1qs = self._sb("w1qs", [128, 4 * HID], FP8)
        self.w2qs = self._sb("w2qs", [128, 2 * 1024], FP8)
        self.w2ld = self._sb("w2ld", [128, 4 * 4096], FP8)
        self.xq3g = self._sb("xq3g", [128, 16 * 128], BF16)
        self.ybuf = self._sb("ybuf", [128, C], F32)

    def dbg_shape(self):
        return {
            "xq1": [128, 16 * NTOK],
            "qkv": [128, 48 * 1024],
            "gdecay": [128, 128],
            "mix0": [128, C],
            "y0": [128, C],
            "yall": [NTOK, C],
            "xq2": [128, 16 * 1024],
            "xq3": [128, 16 * 128],
            "wm": [128, 8],
            "h0": [128, HID],
            "hq0": [128, HID],
        }[self.dbg]

    def col(self, i):
        return self.colA[:, i:i + 1]

    def pcol(self, i):
        return self.colP[:, i:i + 1]

    def rsqrt_col(self, dst, v, eps, sc0, sc1, iters=8, need=()):
        u, s = sc0, sc1
        self.ts(u, v, float(eps), None, OP.add, need=need, dr=True)
        self.ts(s, u, 1.0, 0.5, OP.add, OP.mult, dr=True)
        for _ in range(iters):
            self.cop("vector", lambda e: e.reciprocal(dst, s))
            self.tt(dst, u, dst, OP.mult, dr=True)
            self.tt(s, s, dst, OP.add, dr=True)
            self.ts(s, s, 0.5, None, OP.mult, dr=True)
        self.cop("vector", lambda e: e.reciprocal(dst, s))

    def alias(self, dst_tag, src_tag):
        if src_tag in self.last:
            self.last[dst_tag] = self.last[src_tag]

    def cascade(self, src, acol, u, b, xq_out, extra_add_col=None,
                need_src=(), pfx="", need_out=(), bias_col=None, width=C):
        S, V = "scalar", "vector"
        c1275 = bias_col if bias_col is not None else self.cst[:, 0:1]
        w2 = width // 2
        hs = []
        for hlf, (a0, a1) in enumerate(((0, w2), (w2, width))):
            hs.append((src[:, a0:a1], u[:, a0:a1], b[:, a0:a1],
                       xq_out[:, a0:a1], pfx + f"H{hlf}"))
        for srcH, uH, bH, xqH, px in hs:
            self.op(S, lambda e, srcH=srcH, uH=uH: e.activation(
                uH, srcH, AF.Identity, bias=c1275, scale=acol),
                need=tuple(need_src) + (px + "uw",), tag=px + "u",
                inc=self.sP)
        for j in range(8):
            tb = self.cst[:, 1:2]
            for srcH, uH, bH, xqH, px in hs:
                self.op(S, lambda e, bH=bH, uH=uH, tb=tb: e.activation(
                    bH, uH, AF.Tanh, bias=tb, scale=GBIG),
                    need=(px + "u", px + "bw"), tag=px + "b", inc=self.sP)
            kk = -CAS_K[j] / 2.0
            for srcH, uH, bH, xqH, px in hs:
                self.op(V, lambda e, bH=bH, uH=uH, kk=kk:
                        e.scalar_tensor_tensor(uH, bH, kk, uH, OP.mult,
                                               OP.add),
                        need=(px + "b",), tag=px + "u", inc=self.vP)
                self.mark(px + "bw", self.vP)
        self.alias(pfx + "b", hs[1][4] + "b")
        for srcH, uH, bH, xqH, px in hs:
            if extra_add_col is None:
                self.op(V, lambda e, srcH=srcH, uH=uH, xqH=xqH:
                        e.scalar_tensor_tensor(xqH, srcH, acol, uH, OP.mult,
                                               OP.subtract),
                        need=(px + "u",) + tuple(need_src) + tuple(need_out),
                        tag=pfx + "xq", inc=self.vP)
            else:
                self.op(V, lambda e, srcH=srcH, uH=uH: e.scalar_tensor_tensor(
                    uH, srcH, acol, uH, OP.mult, OP.subtract),
                    need=(px + "u",) + tuple(need_src), tag=px + "t1",
                    inc=self.vP)
                self.ts(xqH, uH, extra_add_col, None, OP.add,
                        need=(px + "t1",) + tuple(need_out), tag=pfx + "xq",
                        inc=self.vP)
            self.mark(px + "uw", self.vP)
        self.mark(pfx + "uw", self.vP)

    def bcast_row_dram(self, dst_tile, row_in, n, need=(), tag=None):
        T, V = "tensor", "vector"
        self.dma("sync", dst_tile[0:1, 0:n], row_in[0:1, 0:n],
                 need=tuple(need), tag=(tag or "bc") + "d")
        for i in range((n + 511) // 512):
            w = min(512, n - i * 512)
            bk = i % 2
            pbk = self.pb[bk]
            rhs = dst_tile[0:1, i * 512:i * 512 + w]
            self.op(T, lambda e, w=w, pbk=pbk, rhs=rhs: e.matmul(
                pbk[:, 0:w], self.onesr[0:1, :], rhs, start=True, stop=True),
                need=((tag or "bc") + "d", "onesr", f"pbfree{bk}"),
                tag=f"pbmm{bk}", inc=self.tP)
            self.op(V, lambda e, i=i, w=w, pbk=pbk: e.tensor_copy(
                dst_tile[:, i * 512:i * 512 + w], pbk[:, 0:w]),
                need=(f"pbmm{bk}",), tag=tag, inc=self.vP)
            self.mark(f"pbfree{bk}", self.vP)

    def colsum_bcast(self, dst_col, src_col, scale, need=(), tag=None):
        T, V = "tensor", "vector"
        onec = self.colA[:, 63:64]
        r11 = self.colA[:, 62:63]
        self.op(T, lambda e: e.matmul(self.pa[2][0:1, 0:1], onec, src_col,
                                      start=True, stop=True),
                need=tuple(need) + ("onec", "pa2free"), tag="pa2mm",
                inc=self.tP)
        self.op(V, lambda e: e.tensor_copy(r11[0:1, 0:1],
                                           self.pa[2][0:1, 0:1]),
                need=("pa2mm",), tag="r11", inc=self.vP, drain=True)
        self.mark("pa2free", self.vP)
        self.op(T, lambda e: e.matmul(self.pa[2][:, 0:1],
                                      self.onesr[0:1, :], r11[0:1, 0:1],
                                      start=True, stop=True),
                need=("r11", "onesr", "pa2free"), tag="pa2mm", inc=self.tP)
        self.op(V, lambda e, scale=scale: e.tensor_scalar(
            dst_col, self.pa[2][:, 0:1], float(scale), None, OP.mult),
            need=("pa2mm",), tag=tag, inc=self.vP, drain=True)
        self.mark("pa2free", self.vP)

    def plan(self):
        self.p0_consts()
        self.p1_means()
        for tc in range(16):
            self.p2_chunk(tc)
        if self.dbg == "xq1":
            self.dma("gpsimd", self.dbg_out[:], self.xq1[:], need=("xq1",),
                     tag="dbg")
        for hf in range(2):
            self.p3_qkv(hf)
            if self.dbg == "qkv" and hf == 0:
                self.dma("gpsimd", self.dbg_out[:, 0:32 * 1024], self.qk[:],
                         need=("qkv",), tag="dbg")
                self.dma("gpsimd", self.dbg_out[:, 32 * 1024:48 * 1024],
                         self.vv[:], need=("qkv",), tag="dbg")
                self.mark("qkv", self.dG)
                self.mark("vvfree", self.dG)
            if self.dbg == "gdecay" and hf == 0:
                self.dma("sync", self.dbg_out[:], self.decay[:], need=("dec",),
                         tag="dbg")
                self.mark("dec", self.dQ)
            for i in range(8):
                self.p4_attn(hf, i)
            self.mark("qkvfree", self.tP)
            if self.dbg == "xq2" and hf == 0:
                self.dma("gpsimd", self.dbg_out[:], self.vv[:], need=("xq2",),
                         tag="dbg")
                self.mark("xq2", self.dG)
            self.p5_setup(hf)
            for i in range(8):
                self.p5_chunk(hf, i)
            self.mark("wpqr", self.tP)
        self.mark("zfree_v", self.vP)
        self.mark("zfree_s", self.sP)
        self.mark("zfree_t", self.tP)
        self.mark("zfree_q", self.dQ)
        self._pop_to("qk")
        self.alloc_p6()
        self.p5x_wquant()
        for g in range(16):
            self.p6_group(g)
        self.fin()

    def p0_consts(self):
        V, Q = "vector", "sync"
        self.dma(Q, self.id16[:], self.eye_in[:], tag="id16")
        self.dma(Q, self.UB[:, UB_MASK:UB_MASK + 128], self.mask_in[:],
                 tag="mask")
        self.dma(Q, self.bqg[:], self.bqg_in[:], tag="bqg")
        self.op(V, lambda e: e.tensor_copy(self.id32[:], self.id16[:]),
                need=("id16",), tag="id32", inc=self.vP, drain=True)
        self.op(V, lambda e: e.memset(self.kv[:], 0.0), tag="kv", inc=self.vP)
        self.op(V, lambda e: e.memset(self.onesr[:, :], 1.0), tag="onesr",
                inc=self.vP, drain=True)
        self.op(V, lambda e: e.memset(self.colA[:, 63:64], 1.0), tag="onec",
                inc=self.vP, drain=True)
        self.op(V, lambda e: e.memset(self.cst[:, 0:1], 0.5), tag="cst",
                inc=self.vP, drain=True)
        self.op(V, lambda e: e.memset(self.cst[:, 1:2], 64.0), tag="cst",
                inc=self.vP, drain=True)
        self.op(V, lambda e: e.memset(self.cst[:, 9:10], -0.5), tag="cst",
                inc=self.vP, drain=True)
        self.op(V, lambda e: e.memset(self.cst[:, 10:11], 0.5), tag="cst",
                inc=self.vP, drain=True)

    def p1_means(self):
        V, Q = "vector", "sync"
        accq, accg, acc1, acc2, tmp = (self.col(0), self.col(1), self.col(2),
                                       self.col(3), self.col(4))
        for c in (accq, accg, acc1, acc2):
            self.op(V, lambda e, c=c: e.memset(c, 0.0), tag="macc",
                    inc=self.vP, drain=True)

        def meanstream(src_ap_fn, nblk, acc, pfx):
            for i in range(nblk):
                slot = i % 2
                stg = self.wstg[:, slot * C:(slot + 1) * C]
                self.dma(Q, stg, src_ap_fn(i), need=(f"{pfx}r{slot}",),
                         tag=f"{pfx}d{slot}")
                self.op(V, lambda e, stg=stg: e.tensor_reduce(
                    tmp, stg, AX.X, OP.add, apply_absolute_value=True),
                    need=(f"{pfx}d{slot}",), tag="mtmp", inc=self.vP,
                    drain=True)
                self.mark(f"{pfx}r{slot}", self.vP)
                self.tt(acc, acc, tmp, OP.add, need=("mtmp", "macc"),
                        tag="macc", inc=self.vP, dr=True)

        meanstream(lambda i: self.wqg_in[i], 48, accq, "mq")
        meanstream(lambda i: self.wqg_in[48 + i], 16, accg, "mg")
        meanstream(lambda i: self.w1t_in[(i // 4) * 128:(i // 4) * 128 + 128,
                                         (i % 4) * C:(i % 4) * C + C],
                   64, acc1, "m1")
        meanstream(lambda i: self.w2t_in[i * 128:(i + 1) * 128, :],
                   64, acc2, "m2")
        self.colsum_bcast(self.pcol(PC_MQ), accq, 1.0 / (6144.0 * 2048.0),
                          need=("macc",), tag="pcmq")
        self.colsum_bcast(self.pcol(PC_MG), accg, 1.0 / (2048.0 * 2048.0),
                          need=("macc",), tag="pcmg")
        self.colsum_bcast(self.pcol(PC_M1), acc1, 1.0 / (8192.0 * 2048.0),
                          need=("macc",), tag="pcm1")
        self.colsum_bcast(self.pcol(PC_M2), acc2, 1.0 / (8192.0 * 2048.0),
                          need=("macc",), tag="pcm2")
        for i, t in ((PC_MQ, "pcmq"), (PC_MG, "pcmg"), (PC_M1, "pcm1"),
                     (PC_M2, "pcm2")):
            self.ts(self.pcol(i), self.pcol(i), 1e-5, None, OP.max,
                    need=(t,), tag=t, inc=self.vP, dr=True)
        self.cop(V, lambda e: e.reciprocal(self.pcol(PC_IM1),
                                           self.pcol(PC_M1)),
                 need=("pcm1",), tag="pcim1", inc=self.vP)
        self.cop(V, lambda e: e.reciprocal(self.pcol(PC_IM2),
                                           self.pcol(PC_M2)),
                 need=("pcm2",), tag="pcim2", inc=self.vP)
        if self.dbg == "wm":
            self.dma(Q, self.dbg_out[:], self.colP[:, 0:8],
                     need=("pcmq", "pcmg", "pcim1", "pcim2"), tag="dbg")

    def p2_chunk(self, tc):
        S, V, T, Q = "scalar", "vector", "tensor", "sync"
        xt = self.wstg[:, 0:C]
        u = self.wstg[:, C:2 * C]
        b = self.wsgn[:, 0:C]
        xqtm = b
        (csum, cssq, cmean, cvar, cistd, cmx, cmn, cxm, cxsc, cq, ca, ccm,
         s0, s1) = [self.col(i) for i in range(8, 22)]
        self.dma(Q, xt, self.x_in[tc * 128:(tc + 1) * 128, :],
                 need=("xw", "mqr0", "m2r0"), tag="xd")
        self.op(V, lambda e: e.tensor_reduce(csum, xt, AX.X, OP.add),
                need=("xd",), tag="csum", inc=self.vP, drain=True)
        self.op(V, lambda e: e.tensor_tensor(u, xt, xt, OP.mult),
                need=("xd", "c2uw", "mqr1", "m2r1"), tag="cssqm", inc=self.vP)
        self.op(V, lambda e: e.tensor_reduce(cssq, u, AX.X, OP.add),
                need=("cssqm",), tag="cssq", inc=self.vP, drain=True)
        self.op(V, lambda e: e.tensor_reduce(cmx, xt, AX.X, OP.max),
                need=("xd",), tag="cmx", inc=self.vP, drain=True)
        self.op(V, lambda e: e.tensor_reduce(cmn, xt, AX.X, OP.min),
                need=("xd",), tag="cmn", inc=self.vP, drain=True)
        self.ts(cmean, csum, 1.0 / C, None, OP.mult, dr=True)
        self.ts(cvar, cssq, 1.0 / C, None, OP.mult, need=("cssq",), dr=True)
        self.tt(s0, cmean, cmean, OP.mult, dr=True)
        self.tt(cvar, cvar, s0, OP.subtract, dr=True)
        self.rsqrt_col(cistd, cvar, 1e-5, s0, s1)
        self.tt(cmx, cmx, cmean, OP.subtract, dr=True)
        self.tt(cmn, cmean, cmn, OP.subtract, dr=True)
        self.tt(cxm, cmx, cmn, OP.max, dr=True)
        self.tt(cxm, cxm, cistd, OP.mult, dr=True)
        self.ts(cxsc, cxm, 1e-5, None, OP.max, dr=True)
        self.cop(V, lambda e: e.reciprocal(cq, cxsc))
        self.ts(cq, cq, 127.0, None, OP.mult, dr=True)
        self.tt(ca, cistd, cq, OP.mult, dr=True)
        self.tt(ccm, cmean, ca, OP.mult, dr=True)
        self.ts(ccm, ccm, -1.0, None, OP.mult, dr=True)
        cb2 = self.col(22)
        self.ts(cb2, ccm, 0.5, None, OP.add, tag="cols1", inc=self.vP,
                dr=True)
        self.op(V, lambda e, tc=tc: e.tensor_copy(
            self.colP[:, PC_XSC1 + tc:PC_XSC1 + tc + 1], cxsc),
            tag="xscsv", inc=self.vP, drain=True)
        self.mark("c2src", self.vP)
        self.cascade(xt, ca, u, b, xqtm, extra_add_col=ccm,
                     need_src=("c2src", "cols1"), pfx="c2", bias_col=cb2)
        self.mark("xw", self.vP)
        xq1v = self.xq1[:].rearrange("p (cc t) -> p cc t", cc=16)
        ptrv = self.ptr[:, 0:512].rearrange("p (k t) -> p k t", k=4)
        for grp in range(4):
            for k in range(4):
                cc = grp * 4 + k
                self.op(T, lambda e, cc=cc, k=k: e.transpose(
                    self.ptr[:, k * 128:(k + 1) * 128],
                    xqtm[:, cc * 128:(cc + 1) * 128], self.id16[:]),
                    need=(("c2xq", "ptrfree", "id16") if k == 0 else ()),
                    tag=("ptrmm" if k == 3 else None),
                    inc=(self.tP if k == 3 else None))
            self.op(V, lambda e, grp=grp, tc=tc: e.tensor_copy(
                xq1v[:, grp * 4:(grp + 1) * 4, tc * 128:(tc + 1) * 128], ptrv),
                need=("ptrmm",), tag="xq1", inc=self.vP)
            self.mark("ptrfree", self.vP)
        self.mark("c2H0bw", self.tP)
        self.mark("c2H1bw", self.tP)
        self.mark("c2xqr", self.tP)

    def p3_qkv(self, hf):
        S, V, T, Q = "scalar", "vector", "tensor", "sync"
        for tch in range(8):
            gtc = hf * 8 + tch
            cxs = self.colP[:, PC_XSC1 + gtc:PC_XSC1 + gtc + 1]
            self.op(T, lambda e, cxs=cxs: e.transpose(
                self.pa[2][0:1, 0:128], cxs, self.id32[:]),
                need=("xscsv", "pa2free", "id32"), tag="pa2mm", inc=self.tP)
            self.op(V, lambda e, tch=tch: e.tensor_copy(
                self.outy[0:1, tch * 128:(tch + 1) * 128],
                self.pa[2][0:1, 0:128]),
                need=("pa2mm", "outyr", "m2r0", "m2r1"), tag="xsrow",
                inc=self.vP, drain=True)
            self.mark("pa2free", self.vP)
        def build_s1(mcol, needr):
            for blk in range(2):
                self.op(T, lambda e, blk=blk: e.matmul(
                    self.pb[blk][:, 0:512], self.onesr[0:1, :],
                    self.outy[0:1, blk * 512:(blk + 1) * 512],
                    start=True, stop=True),
                    need=("xsrow", "onesr", f"pbfree{blk}", "s1br"),
                    tag=f"pbmm{blk}", inc=self.tP)
                self.op(V, lambda e, blk=blk, mcol=mcol: e.tensor_scalar(
                    self.s1b[:, blk * 512:(blk + 1) * 512],
                    self.pb[blk][:, 0:512], mcol, 1.0 / 127.0,
                    OP.mult, op1=OP.mult),
                    need=(f"pbmm{blk}", needr, "s1br", "pTmm", "atr", "eex"),
                    tag="s1b", inc=self.vP)
                self.mark(f"pbfree{blk}", self.vP)
        build_s1(self.pcol(PC_MQ), "pcmq")
        self.mark("gevr", self.tP)

        gv = self.outy
        for oc in range(64):
            if oc == 48:
                self.mark("s1br", self.vP)
                build_s1(self.pcol(PC_MG), "pcmg")
            slot = oc % 2
            stg = self.wstg[:, slot * C:(slot + 1) * C]
            sgn = self.wsgn[:, slot * C:(slot + 1) * C]
            self.dma(Q, stg, self.wqg_in[oc],
                     need=(f"wsg{slot}", "xw", "c2uw", "c5uw"),
                     tag=f"wd{slot}")
            self.op(S, lambda e, stg=stg, sgn=sgn: e.activation(
                sgn, stg, AF.Sign),
                need=(f"wd{slot}", f"wsgr{slot}", "c2xq", "c2b", "c2xqr",
                      "c4xq", "c4b", "c4xqr", "c5xq", "c5b", "c5xqr",
                      "packr", "atr", "pbhmm0", "pbhmm1"),
                tag=f"wsg{slot}", inc=self.sP)
            for ts_ in range(2):
                bank = ts_ % 2
                pbk = self.pb[bank]
                t0 = hf * 1024 + ts_ * 512
                for cc in range(16):
                    first, lastm = cc == 0, cc == 15
                    src = self.xq1[:, cc * NTOK + t0:cc * NTOK + t0 + 512]
                    lhs = sgn[:, cc * 128:(cc + 1) * 128]
                    self.op(T, lambda e, lhs=lhs, src=src, pbk=pbk,
                            first=first, lastm=lastm: e.matmul(
                                pbk[:, 0:512], lhs, src,
                                start=first, stop=lastm),
                            need=((f"wsg{slot}", "xq1", f"pbfree{bank}")
                                  if first else ()),
                            tag=(f"pbmm{bank}" if lastm else None),
                            inc=(self.tP if lastm else None))
                self.mark(f"wsgr{slot}", self.tP)
                if oc < 48:
                    if oc < 32:
                        dst = self.qk[:, oc * 1024 + ts_ * 512:
                                      oc * 1024 + ts_ * 512 + 512]
                        ndst = ("wpqr",)
                    else:
                        dst = self.vv[:, (oc - 32) * 1024 + ts_ * 512:
                                      (oc - 32) * 1024 + ts_ * 512 + 512]
                        ndst = ("vvfree",)
                    sb_ = self.s1b[:, ts_ * 512:(ts_ + 1) * 512]
                    self.tt(dst, pbk[:, 0:512], sb_, OP.mult,
                            need=(f"pbmm{bank}", "s1b") + ndst, tag="qkvm",
                            inc=self.vP)
                    self.ts(dst, dst, self.bqg[:, oc:oc + 1], None, OP.add,
                            need=("bqg",), tag="qkv", inc=self.vP)
                    self.mark(f"pbfree{bank}", self.vP)
                else:
                    sb_ = self.s1b[:, ts_ * 512:(ts_ + 1) * 512]
                    gdst = gv[:, ts_ * 512:(ts_ + 1) * 512]
                    self.tt(gdst, pbk[:, 0:512], sb_, OP.mult,
                            need=(f"pbmm{bank}", "s1bg", "gevr"), tag="gmul",
                            inc=self.vP, dr=True)
                    self.mark(f"pbfree{bank}", self.vP)
                    self.op(S, lambda e, gdst=gdst, oc=oc: e.activation(
                        gdst, gdst, AF.Sigmoid, bias=self.bqg[:, oc:oc + 1]),
                        need=("gmul", "bqg"), tag="gsig", inc=self.sP,
                        drain=True)
            if oc == 48:
                pass
            if oc >= 48:
                h = oc - 48
                gvv = gv[:, 0:1024].rearrange("p (s q) -> p s q", s=8)
                self.op(V, lambda e, h=h, gvv=gvv: e.tensor_reduce(
                    self.decay[:, h * 8:h * 8 + 8], gvv, AX.X, OP.add),
                    need=("gsig",), tag="dec0", inc=self.vP, drain=True)
                self.op(V, lambda e, h=h: e.tensor_scalar(
                    self.decay[:, h * 8:h * 8 + 8],
                    self.decay[:, h * 8:h * 8 + 8], 1.0 / 128.0, None,
                    OP.mult), need=("dec0",), tag="dec", inc=self.vP,
                    drain=True)
                self.mark("gevr", self.vP)
        self.mark("outyr", self.vP)

    def p4_attn(self, hf, i):
        S, V, T = "scalar", "vector", "tensor"
        cR, cRb = self.col(24), self.col(25)
        rs = self.colA[:, 37:41]
        qp = self.UB[:, UB_QP:UB_QP + 512]
        kp = self.UB[:, UB_KP:UB_KP + 512]
        vp = self.UB[:, UB_VP:UB_VP + 512]
        vt = self.UB[:, UB_VT:UB_VT + 512]
        lb = self.UB[:, UB_LB:UB_LB + 2048]
        eex = self.UB[:, UB_EEX:UB_EEX + 512]
        at = self.UB[:, UB_AT:UB_AT + 512]
        mask8 = self.UB[:, UB_MASK:UB_MASK + 128]
        qhv = self.qk[:, 0:16 * 1024].rearrange("p (h t) -> p h t", h=16)
        khv = self.qk[:, 16 * 1024:32 * 1024].rearrange(
            "p (h t) -> p h t", h=16)
        vhv = self.vv[:].rearrange("p (h t) -> p h t", h=16)
        pdst = lambda t_: t_.rearrange("p (tb h l) -> p tb h l", tb=4, h=16)
        lbhv = lb.rearrange("p (h t) -> p h t", h=16)
        pb3v = self.pb[3][:].rearrange("p (tb h l) -> p tb h l", tb=4, h=16)
        for g in range(4):
            off = i * 128 + g * 32
            for nm, dst, srcv in (("q", qp, qhv), ("k", kp, khv),
                                  ("v", vp, vhv)):
                sv = srcv[:, :, off:off + 32].rearrange(
                    "p h (tb l) -> p tb h l", l=8)
                self.op(V, lambda e, dst=dst, sv=sv: e.tensor_copy(
                    pdst(dst), sv),
                    need=(("qkv", "packr", "c2b", "c2xqr", "c4b", "c4xqr",
                           "c5b", "c5xqr") if nm == "q" else ()),
                    tag=("pack" if nm == "v" else None),
                    inc=(self.vP if nm == "v" else None), drain=(nm == "v"))
            for b4 in range(4):
                first, last = b4 == 0, b4 == 3
                self.op(T, lambda e, b4=b4: e.matmul(
                    self.pb[0][:, b4 * 128:(b4 + 1) * 128],
                    qp[:, b4 * 128:(b4 + 1) * 128],
                    kp[:, b4 * 128:(b4 + 1) * 128], start=True, stop=True),
                    need=(("pack", "pbfree0") if first else ()),
                    tag=("Emm" if last else None),
                    inc=(self.tP if last else None))
            self.op(V, lambda e: e.tensor_reduce(
                cR, self.pb[0][:, 0:512], AX.X, OP.max),
                need=("Emm",), tag="cRr", inc=self.vP, drain=True)
            self.ts(cRb, cR, -SC_ATT, None, OP.mult, need=("cRr",),
                    tag="cRb", inc=self.vP, dr=True)
            self.op(S, lambda e: e.activation(
                eex, self.pb[0][:, 0:512], AF.Exp, bias=cRb, scale=SC_ATT),
                need=("cRb", "eexr", "qkvm", "gmul"), tag="eex", inc=self.sP,
                drain=True)
            self.mark("pbfree0", self.sP)
            for b4 in range(4):
                self.tt(eex[:, b4 * 128:(b4 + 1) * 128],
                        eex[:, b4 * 128:(b4 + 1) * 128], mask8, OP.mult,
                        need=(("eex", "mask") if b4 == 0 else ()),
                        dr=(b4 == 3))
            self.op(V, lambda e: e.tensor_reduce(
                rs, eex.rearrange("p (b n) -> p b n", b=4), AX.X, OP.add),
                drain=True)
            self.cop(V, lambda e: e.reciprocal(rs, rs))
            for b4 in range(4):
                self.ts(eex[:, b4 * 128:(b4 + 1) * 128],
                        eex[:, b4 * 128:(b4 + 1) * 128],
                        rs[:, b4:b4 + 1], None, OP.mult,
                        tag=("anrm" if b4 == 3 else None),
                        inc=(self.vP if b4 == 3 else None), dr=(b4 == 3))
            for b4 in range(4):
                first, last = b4 == 0, b4 == 3
                self.op(T, lambda e, b4=b4: e.transpose(
                    self.ptr[:, b4 * 128:(b4 + 1) * 128],
                    eex[:, b4 * 128:(b4 + 1) * 128], self.id16[:]),
                    need=(("anrm", "ptrfree", "id16") if first else ()),
                    tag=("pTmm" if last else None),
                    inc=(self.tP if last else None))
            self.mark("eexr", self.tP)
            self.op(V, lambda e: e.tensor_copy(at, self.ptr[:, 0:512]),
                    need=("pTmm", "atr", "qkvm", "gmul"), tag="atc",
                    inc=self.vP, drain=True)
            for b4 in range(4):
                first, last = b4 == 0, b4 == 3
                self.op(T, lambda e, b4=b4: e.transpose(
                    self.ptr[:, 512 + b4 * 128:512 + (b4 + 1) * 128],
                    vp[:, b4 * 128:(b4 + 1) * 128], self.id16[:]),
                    need=(("atc",) if first else ()),
                    tag=("pVmm" if last else None),
                    inc=(self.tP if last else None))
            self.mark("packr", self.tP)
            self.op(V, lambda e: e.tensor_copy(vt, self.ptr[:, 512:1024]),
                    need=("pVmm",), tag="vtc", inc=self.vP, drain=True)
            self.mark("ptrfree", self.vP)
            for b4 in range(4):
                first, last = b4 == 0, b4 == 3
                self.op(T, lambda e, b4=b4: e.matmul(
                    self.pb[3][:, b4 * 128:(b4 + 1) * 128],
                    vt[:, b4 * 128:(b4 + 1) * 128],
                    at[:, b4 * 128:(b4 + 1) * 128], start=True, stop=True),
                    need=(("atc", "vtc", "pbfree3") if first else ()),
                    tag=("pLmm" if last else None),
                    inc=(self.tP if last else None))
            self.mark("atr", self.tP)
            lbg = lbhv[:, :, g * 32:g * 32 + 32].rearrange(
                "p h (tb l) -> p tb h l", l=8)
            self.op(V, lambda e, lbg=lbg: e.tensor_copy(lbg, pb3v),
                    need=("pLmm", "lbr", "wsgr1"), tag="lbw", inc=self.vP,
                    drain=True)
            self.mark("pbfree3", self.vP)
        kv16h = self.ktm[:, KT_KV0:KT_KV0 + 128]
        for h in range(H):
            r, j = h // 4, h % 4
            bank = self.pb[1 + (r % 2)]
            qf = self.qk[:, h * 1024 + i * 128:h * 1024 + (i + 1) * 128]
            kf = self.qk[:, (16 + h) * 1024 + i * 128:
                         (16 + h) * 1024 + (i + 1) * 128]
            vf = self.vv[:, h * 1024 + i * 128:h * 1024 + (i + 1) * 128]
            kvslice = self.kv[:, h * 128:(h + 1) * 128]
            self.op(V, lambda e, kvslice=kvslice: e.tensor_copy(
                kv16h, kvslice),
                need=("kv", "kvsR"), tag="kvs", inc=self.vP, drain=True)
            self.op(T, lambda e, bank=bank, j=j, h=h: e.matmul(
                bank[:, j * 128:(j + 1) * 128],
                lb[:, h * 128:(h + 1) * 128], self.id16[:],
                start=True, stop=False),
                need=(("lbw", f"pbfree{1 + (r % 2)}", "id16")
                      if j == 0 else ()))
            self.op(T, lambda e, bank=bank, j=j, qf=qf: e.matmul(
                bank[:, j * 128:(j + 1) * 128], qf, kv16h,
                start=False, stop=True),
                need=("kvs",), tag=f"pbhmm{r % 2}", inc=self.tP)
            self.mark("kvsR", self.tP)
            self.op(T, lambda e, kf=kf: e.transpose(
                self.ptr[:, 0:128], kf, self.id16[:]),
                need=("qkv", "ptrfree", "id16"))
            self.op(T, lambda e, vf=vf: e.transpose(
                self.ptr[:, 128:256], vf, self.id16[:]),
                tag="ptrmm", inc=self.tP)
            self.op(V, lambda e: e.tensor_copy(self.ktm[:, 0:256],
                                               self.ptr[:, 0:256]),
                    need=("ptrmm", "ktmr"), tag="ktm", inc=self.vP,
                    drain=True)
            self.mark("ptrfree", self.vP)
            self.op(T, lambda e: e.matmul(
                self.pa[2][:, 0:128], self.ktm[:, 0:128],
                self.ktm[:, 128:256], start=True, stop=True),
                need=("ktm", "pa2free"), tag="pa2mm", inc=self.tP)
            self.mark("ktmr", self.tP)
            dcol = self.decay[:, h * 8 + i:h * 8 + i + 1]
            self.op(V, lambda e, kvslice=kvslice, dcol=dcol:
                    e.scalar_tensor_tensor(
                        kvslice, kvslice, dcol, self.pa[2][:, 0:128],
                        OP.mult, OP.add),
                    need=("pa2mm", "dec", "kvs"), tag="kv", inc=self.vP)
            self.mark("pa2free", self.vP)
            if j == 3:
                for j2 in range(4):
                    h2 = r * 4 + j2
                    self.op(V, lambda e, h2=h2, j2=j2, bank=bank:
                            e.tensor_copy(
                                self.outy[:, h2 * 128:(h2 + 1) * 128],
                                bank[:, j2 * 128:(j2 + 1) * 128]),
                            need=((f"pbhmm{r % 2}", "outyr")
                                  if j2 == 0 else ()),
                            tag=("outcp" if j2 == 3 else None),
                            inc=(self.vP if j2 == 3 else None))
                self.mark(f"pbfree{1 + (r % 2)}", self.vP)
                self.mark("lbr", self.vP)
        if self.dbg == "mix0" and hf == 0 and i == 0:
            self.dma("sync", self.dbg_out[:], self.outy[:], need=("outcp",),
                     tag="dbg")
            self.mark("outcp", self.dQ)
        u = self.wstg[:, C:2 * C]
        bts = self.wsgn[:, 0:C]
        xqtm = bts
        (cS2, cms, crs, cM2, cxsc, cq, ca, s0, s1) = [
            self.col(j) for j in range(28, 37)]
        sq = self.wstg[:, 0:C]
        self.op(V, lambda e, sq=sq: e.tensor_tensor(
            sq, self.outy[:], self.outy[:], OP.mult),
            need=("outcp", "xw", "xd2"), tag="cS2m", inc=self.vP)
        self.op(V, lambda e, sq=sq: e.tensor_reduce(cS2, sq, AX.X, OP.add),
                need=("cS2m",), tag="cS2", inc=self.vP, drain=True)
        self.bcast_row_dram(u, self.nw_in, C,
                            need=("c4uw", "c2uw", "wd0", "wd1", "wsg0",
                                  "wsg1"), tag="nwu")
        self.tt(self.outy[:], self.outy[:], u, OP.mult,
                need=("outcp", "nwu", "cS2m"), tag="outn", inc=self.vP)
        self.mark("c4uw", self.vP)
        self.ts(cms, cS2, 1.0 / C, None, OP.mult, need=("cS2",), dr=True)
        self.rsqrt_col(crs, cms, 1e-6, s0, s1)
        self.op(V, lambda e: e.tensor_reduce(
            cM2, self.outy[:], AX.X, OP.max, apply_absolute_value=True),
            need=("outn",), tag="cM2v", inc=self.vP, drain=True)
        self.tt(cM2, cM2, crs, OP.mult, dr=True)
        self.ts(cxsc, cM2, 1e-5, None, OP.max, dr=True)
        self.cop(V, lambda e: e.reciprocal(cq, cxsc))
        self.ts(cq, cq, 127.0, None, OP.mult, dr=True)
        self.tt(ca, crs, cq, OP.mult, tag="cols2", inc=self.vP, dr=True)
        gi = hf * 8 + i
        self.op(V, lambda e, gi=gi: e.tensor_copy(
            self.colP[:, PC_XSC2 + gi:PC_XSC2 + gi + 1], cxsc),
            tag="xsc2sv", inc=self.vP, drain=True)
        self.mark("c4src", self.vP)
        self.cascade(self.outy[:], ca, u, bts, xqtm,
                     need_src=("c4src", "cols2"), pfx="c4")
        self.mark("outyr", self.vP)
        xq2v = self.vv[:].rearrange("p (cc t) -> p cc t", cc=16)
        ptrv = self.ptr[:, 0:512].rearrange("p (k t) -> p k t", k=4)
        for grp in range(4):
            for k in range(4):
                cc = grp * 4 + k
                self.op(T, lambda e, cc=cc, k=k: e.transpose(
                    self.ptr[:, k * 128:(k + 1) * 128],
                    xqtm[:, cc * 128:(cc + 1) * 128], self.id16[:]),
                    need=(("c4xq", "ptrfree") if k == 0 else ()),
                    tag=("ptrmm" if k == 3 else None),
                    inc=(self.tP if k == 3 else None))
            self.op(V, lambda e, grp=grp, i=i: e.tensor_copy(
                xq2v[:, grp * 4:(grp + 1) * 4, i * 128:(i + 1) * 128], ptrv),
                need=("ptrmm", "pbhmm0", "pbhmm1", "vvfree"), tag="xq2",
                inc=self.vP)
            self.mark("ptrfree", self.vP)
        self.mark("c4H0bw", self.tP)
        self.mark("c4H1bw", self.tP)
        self.mark("c4xqr", self.tP)

    def p5_setup(self, hf):
        S, V, Q = "scalar", "vector", "sync"
        accp, tmp = self.col(40), self.col(41)
        self.op(V, lambda e: e.memset(accp, 0.0), tag="pacc", inc=self.vP,
                drain=True)
        for pc in range(16):
            slot = pc % 2
            stg = self.wstg[:, slot * C:(slot + 1) * C]
            self.dma(Q, stg, self.wp_in[pc],
                     need=(f"wpsg{slot}", "xw", "c4uw", "mpv"),
                     tag=f"wd{slot}")
            dst = self.qk[:, pc * C:(pc + 1) * C]
            self.op(S, lambda e, stg=stg, dst=dst: e.activation(
                dst, stg, AF.Sign),
                need=(f"wd{slot}", "qkvfree", "c4xq", "c4b", "c4xqr"),
                tag=f"wpsg{slot}", inc=self.sP)
            self.op(V, lambda e, stg=stg: e.tensor_reduce(
                tmp, stg, AX.X, OP.add, apply_absolute_value=True),
                need=(f"wd{slot}",), tag="ptmp", inc=self.vP, drain=True)
            self.mark("mpv", self.vP)
            self.tt(accp, accp, tmp, OP.add, need=("ptmp", "pacc"),
                    tag="pacc", inc=self.vP, dr=True)
        self.mark("wpq", self.sP)
        self.colsum_bcast(self.pcol(PC_MP), accp, 1.0 / (2048.0 * 2048.0),
                          need=("pacc",), tag="pcmp")
        self.ts(self.pcol(PC_MP), self.pcol(PC_MP), 1e-5, None, OP.max,
                need=("pcmp",), tag="pcmp", inc=self.vP, dr=True)

    def p5_chunk(self, hf, i):
        S, V, T, Q = "scalar", "vector", "tensor", "sync"
        gi = hf * 8 + i
        cs2, t0c = self.col(42), self.col(43)
        self.tt(t0c, self.colP[:, PC_XSC2 + gi:PC_XSC2 + gi + 1],
                self.pcol(PC_MP), OP.mult, need=("xsc2sv", "pcmp"), dr=True)
        self.ts(cs2, t0c, 1.0 / 127.0, None, OP.mult, tag="cs2", inc=self.vP,
                dr=True)
        for os_ in range(4):
            bank = os_ % 2
            pbk = self.pb[bank]
            for cc in range(16):
                first, lastm = cc == 0, cc == 15
                lhs = self.vv[:, cc * 1024 + i * 128:cc * 1024 + (i + 1) * 128]
                rhs = self.qk[:, cc * C + os_ * 512:cc * C + os_ * 512 + 512]
                self.op(T, lambda e, lhs=lhs, rhs=rhs, pbk=pbk, first=first,
                        lastm=lastm: e.matmul(pbk[:, 0:512], lhs, rhs,
                                              start=first, stop=lastm),
                        need=((f"pbfree{bank}", "wpq", "xq2") if first
                              else ()),
                        tag=(f"pbmm{bank}" if lastm else None),
                        inc=(self.tP if lastm else None))
            self.op(S, lambda e, os_=os_, pbk=pbk: e.activation(
                self.outy[:, os_ * 512:(os_ + 1) * 512], pbk[:, 0:512],
                AF.Identity, scale=cs2),
                need=(f"pbmm{bank}", "cs2", "outyr"), tag="proj", inc=self.sP,
                drain=True)
            self.mark(f"pbfree{bank}", self.sP)
        u = self.wstg[:, C:2 * C]
        self.bcast_row_dram(u, self.bp_in, C,
                            need=("c5uw", "c4uw", "c2uw", "wd0", "wd1",
                                  "wsg0", "wsg1", "wpsg0", "wpsg1"),
                            tag="bpu")
        self.tt(self.outy[:], self.outy[:], u, OP.add, need=("proj", "bpu"),
                tag="yb", inc=self.vP)
        self.mark("c5uw", self.vP)
        xt = self.wstg[:, 0:C]
        self.dma(Q, xt, self.x_in[gi * 128:(gi + 1) * 128, :],
                 need=("xw", "wd0", "wd1", "cS2"), tag="xd2")
        self.tt(self.outy[:], self.outy[:], xt, OP.add,
                need=("yb", "xd2"), tag="yrow", inc=self.vP)
        self.dma(Q, self.y_scr[gi * 128:(gi + 1) * 128, :], self.outy[:],
                 need=("yrow",), tag="yst")
        self.mark("outyr", self.dQ)
        if self.dbg == "y0" and gi == 0:
            self.dma(Q, self.dbg_out[:], self.outy[:], need=("yrow",),
                     tag="dbg")
            self.mark("yrow", self.dQ)
        bts = self.wsgn[:, 0:C]
        xqtm = bts
        (csum, cssq, cmean, cvar, cistd, cw, crs2, czs, czb, cM3, cxsc, cq,
         ca, s0, s1) = [self.col(j) for j in range(44, 59)]
        self.op(V, lambda e: e.tensor_reduce(csum, self.outy[:], AX.X,
                                             OP.add),
                need=("yrow",), tag="c5sum", inc=self.vP, drain=True)
        self.op(V, lambda e: e.tensor_tensor(u, self.outy[:], self.outy[:],
                                             OP.mult),
                need=("yrow", "c5uw"), tag="c5ssqm", inc=self.vP)
        self.op(V, lambda e: e.tensor_reduce(cssq, u, AX.X, OP.add),
                need=("c5ssqm",), tag="c5ssq", inc=self.vP, drain=True)
        self.ts(cmean, csum, 1.0 / C, None, OP.mult, need=("c5sum",), dr=True)
        self.ts(cvar, cssq, 1.0 / C, None, OP.mult, need=("c5ssq",), dr=True)
        self.tt(s0, cmean, cmean, OP.mult, dr=True)
        self.tt(cvar, cvar, s0, OP.subtract, dr=True)
        self.rsqrt_col(cistd, cvar, 1e-5, s0, s1)
        self.tt(cw, cvar, cistd, OP.mult, dr=True)
        self.tt(cw, cw, cistd, OP.mult, dr=True)
        self.rsqrt_col(crs2, cw, 1e-6, s0, s1)
        self.tt(czs, cistd, crs2, OP.mult, dr=True)
        self.tt(czb, cmean, czs, OP.mult, dr=True)
        self.ts(czb, czb, -1.0, None, OP.mult, tag="c5zb", inc=self.vP,
                dr=True)
        self.op(S, lambda e: e.activation(xt, self.outy[:], AF.Identity,
                                          bias=czb, scale=czs),
                need=("c5zb", "yrow", "xw"), tag="c5z", inc=self.sP)
        self.mark("c5uw2", self.sP)
        self.bcast_row_dram(u, self.gam_in, C,
                            need=("c5uw", "c5ssq"), tag="gamu")
        self.tt(xt, xt, u, OP.mult, need=("c5z", "gamu"), tag="c5zg",
                inc=self.vP)
        self.mark("c5uw", self.vP)
        self.op(V, lambda e: e.tensor_reduce(
            cM3, xt, AX.X, OP.max, apply_absolute_value=True),
            need=("c5zg",), tag="c5m", inc=self.vP, drain=True)
        self.ts(cxsc, cM3, 1e-5, None, OP.max, dr=True)
        self.cop(V, lambda e: e.reciprocal(cq, cxsc))
        self.ts(ca, cq, 127.0, None, OP.mult, tag="cols3", inc=self.vP,
                dr=True)
        self.op(V, lambda e, gi=gi: e.tensor_copy(
            self.colP[:, PC_XSC3 + gi:PC_XSC3 + gi + 1], cxsc),
            tag="xsc3sv", inc=self.vP, drain=True)
        self.mark("c5src", self.vP)
        self.cascade(xt, ca, u, bts, xqtm,
                     need_src=("c5src", "cols3", "c5zg"), pfx="c5")
        self.mark("xw", self.vP)
        xq3v = self.vv[:].rearrange("p (cc t) -> p cc t", cc=16)
        ptrv = self.ptr[:, 0:512].rearrange("p (k t) -> p k t", k=4)
        for grp in range(4):
            for k in range(4):
                cc = grp * 4 + k
                self.op(T, lambda e, cc=cc, k=k: e.transpose(
                    self.ptr[:, k * 128:(k + 1) * 128],
                    xqtm[:, cc * 128:(cc + 1) * 128], self.id16[:]),
                    need=(("c5xq", "ptrfree") if k == 0 else ()),
                    tag=("ptrmm" if k == 3 else None),
                    inc=(self.tP if k == 3 else None))
            self.op(V, lambda e, grp=grp, i=i: e.tensor_copy(
                xq3v[:, grp * 4:(grp + 1) * 4, i * 128:(i + 1) * 128], ptrv),
                need=("ptrmm", "pbmm0", "pbmm1", "proj"), tag="xq3f",
                inc=self.vP, drain=True)
            self.mark("ptrfree", self.vP)
        self.mark("c5H0bw", self.tP)
        self.mark("c5H1bw", self.tP)
        self.mark("c5xqr", self.tP)
        for cc in range(16):
            self.dma(Q, self.xq3_scr[cc * 128:(cc + 1) * 128,
                                     gi * 128:(gi + 1) * 128],
                     self.vv[:, cc * 1024 + i * 128:cc * 1024 + (i + 1) * 128],
                     need=("xq3f",), tag="xq3st")
        self.mark("vvfree", self.dQ)
        if self.dbg == "xq3" and gi == 0:
            dv = self.dbg_out[:].rearrange("p (cc t) -> p cc t", cc=16)
            sv = self.vv[:].rearrange("p (cc t) -> p cc t", cc=16)
            self.dma("gpsimd", dv, sv[:, :, 0:128], need=("xq3f",), tag="dbg")
            self.mark("vvfree", self.dG)

    def p5x_wquant(self):
        S, V, Q = "scalar", "vector", "sync"
        zf = ("zfree_v", "zfree_s", "zfree_t", "zfree_q")
        for rc in range(16):
            self.dma(Q, self.hbuf[:], self.w1t_in[rc * 128:(rc + 1) * 128, :],
                     need=("qh1r",) + zf, tag="qh1d")
            self.op(S, lambda e: e.activation(
                self.hqtm[:], self.hbuf[:], AF.Sign, bias=self.cst[:, 9:10],
                scale=self.pcol(PC_IM1)),
                need=("qh1d", "pcim1", "hqtmw") + zf, tag="wq1a", inc=self.sP)
            self.op(S, lambda e: e.activation(
                self.hqfm[:], self.hbuf[:], AF.Sign, bias=self.cst[:, 10:11],
                scale=self.pcol(PC_IM1)),
                need=("qh1d", "hqfmw") + zf, tag="wq1b", inc=self.sP)
            self.mark("qh1r", self.sP)
            self.op(V, lambda e: e.tensor_tensor(
                self.w1qs[:, 0:HID], self.hqtm[:], self.hqfm[:], OP.add),
                need=("wq1a", "wq1b", "w1qsr") + zf, tag="w1q8", inc=self.vP)
            self.mark("hqtmw", self.vP)
            self.mark("hqfmw", self.vP)
            for hs in range(16):
                self.dma(Q, self.w1q_scr[hs * C + rc * 128:
                                         hs * C + (rc + 1) * 128, :],
                         self.w1qs[:, hs * 512:(hs + 1) * 512],
                         need=("w1q8",), tag="w1qst")
            self.mark("w1qsr", self.dQ)
        for rc in range(64):
            self.dma(Q, self.hbuf[:, 0:C],
                     self.w2t_in[rc * 128:(rc + 1) * 128, :],
                     need=("qh2r",), tag="qh2d")
            self.op(S, lambda e: e.activation(
                self.hqtm[:, 0:C], self.hbuf[:, 0:C], AF.Sign,
                bias=self.cst[:, 9:10], scale=self.pcol(PC_IM2)),
                need=("qh2d", "pcim2", "hqtmw"), tag="wq2a", inc=self.sP)
            self.op(S, lambda e: e.activation(
                self.hqfm[:, 0:C], self.hbuf[:, 0:C], AF.Sign,
                bias=self.cst[:, 10:11], scale=self.pcol(PC_IM2)),
                need=("qh2d", "hqfmw"), tag="wq2b", inc=self.sP)
            self.mark("qh2r", self.sP)
            self.op(V, lambda e: e.tensor_tensor(
                self.w2qs[:, 0:C], self.hqtm[:, 0:C], self.hqfm[:, 0:C],
                OP.add),
                need=("wq2a", "wq2b", "w2qsr"), tag="w2q8", inc=self.vP)
            self.mark("hqtmw", self.vP)
            self.mark("hqfmw", self.vP)
            self.dma(Q, self.w2q_scr[rc * 128:(rc + 1) * 128, :],
                     self.w2qs[:, 0:C], need=("w2q8",), tag="w2qst")
            self.mark("w2qsr", self.dQ)

    def p6_group(self, g):
        S, V, T, Q = "scalar", "vector", "tensor", "sync"
        cs3, cs4, ca4, cxsc4, cq4, t0c = [self.col(j) for j in
                                          (8, 9, 10, 11, 12, 13)]
        xsrc = self.xq3_scr.rearrange("(cc p) t -> p cc t", p=128)
        self.dma("scalar",
                 self.xq3g[:].rearrange("p (cc t) -> p cc t", cc=16),
                 xsrc[:, :, g * 128:(g + 1) * 128],
                 need=("xq3gr", "xq3st", "zfree_v", "zfree_s", "zfree_t"),
                 tag="xq3gd")
        self.tt(t0c, self.colP[:, PC_XSC3 + g:PC_XSC3 + g + 1],
                self.pcol(PC_M1), OP.mult, need=("xsc3sv", "pcm1"), dr=True)
        self.ts(cs3, t0c, 1.0 / 254.0, None, OP.mult, tag="cs3", inc=self.vP,
                dr=True)
        for hs in range(16):
            slot = hs % 4
            wsl = self.w1qs[:, slot * HID:(slot + 1) * HID]
            src3 = self.w1q_scr[hs * C:(hs + 1) * C, :].rearrange(
                "(cc p) h -> p cc h", p=128)
            dst3 = wsl.rearrange("p (cc h) -> p cc h", cc=16)
            self.dma(Q, dst3, src3, need=(f"w1r{slot}", "w1qst"),
                     tag=f"w1d{slot}")
            bank = hs % 2
            pbk = self.pb[bank]
            for cc in range(16):
                first, lastm = cc == 0, cc == 15
                lhs = self.xq3g[:, cc * 128:(cc + 1) * 128]
                rhs = wsl[:, cc * 512:(cc + 1) * 512]
                self.op(T, lambda e, lhs=lhs, rhs=rhs, pbk=pbk, first=first,
                        lastm=lastm: e.matmul(pbk[:, 0:512], lhs, rhs,
                                              start=first, stop=lastm),
                        need=((f"w1d{slot}", "xq3gd", f"pbfree{bank}")
                              if first else ()),
                        tag=(f"pbmm{bank}" if lastm else None),
                        inc=(self.tP if lastm else None))
            self.mark(f"w1r{slot}", self.tP)
            self.op(S, lambda e, hs=hs, pbk=pbk: e.activation(
                self.hbuf[:, hs * 512:(hs + 1) * 512], pbk[:, 0:512],
                AF.Gelu, scale=cs3),
                need=(f"pbmm{bank}", "cs3", "hbw"), tag="hb", inc=self.sP,
                drain=True)
            self.mark(f"pbfree{bank}", self.sP)
        if self.dbg == "h0" and g == 0:
            self.dma(Q, self.dbg_out[:], self.hbuf[:], need=("hb",),
                     tag="dbg")
            self.mark("hb", self.dQ)
        self.op(V, lambda e: e.tensor_reduce(
            cxsc4, self.hbuf[:], AX.X, OP.max, apply_absolute_value=True),
            need=("hb",), tag="c6m", inc=self.vP, drain=True)
        self.ts(cxsc4, cxsc4, 1e-5, None, OP.max, dr=True)
        self.cop(V, lambda e: e.reciprocal(cq4, cxsc4))
        self.ts(ca4, cq4, 127.0, None, OP.mult, tag="cols4", inc=self.vP,
                dr=True)
        self.tt(t0c, cxsc4, self.pcol(PC_M2), OP.mult, need=("pcm2",),
                dr=True)
        self.ts(cs4, t0c, 1.0 / 254.0, None, OP.mult, tag="cs4", inc=self.vP,
                dr=True)
        for half in range(2):
            hsrc = self.hbuf[:, half * 4096:(half + 1) * 4096]
            hdst = self.hqtm[:, half * 4096:(half + 1) * 4096]
            self.cascade(hsrc, ca4, self.hrun[:], self.hb16[:], hdst,
                         need_src=("c6m", "cols4"), pfx="c6",
                         need_out=("hqtmw",), width=4096)
        self.mark("hbw", self.vP)
        for grp in range(16):
            for k in range(4):
                hc = grp * 4 + k
                self.op(T, lambda e, hc=hc, k=k: e.transpose(
                    self.ptr[:, k * 128:(k + 1) * 128],
                    self.hqtm[:, hc * 128:(hc + 1) * 128], self.id16[:]),
                    need=(("c6xq", "ptrfree") if k == 0 else ()),
                    tag=("ptrmm" if k == 3 else None),
                    inc=(self.tP if k == 3 else None))
            self.op(V, lambda e, grp=grp: e.tensor_copy(
                self.hqfm[:, grp * 512:(grp + 1) * 512], self.ptr[:, 0:512]),
                need=("ptrmm", "hqfr"), tag="hqf", inc=self.vP, drain=True)
            self.mark("ptrfree", self.vP)
        self.mark("hqtmw", self.vP)
        if self.dbg == "hq0" and g == 0:
            self.dma("gpsimd", self.dbg_out[:], self.hqfm[:], need=("hqf",),
                     tag="dbg")
            self.mark("hqf", self.dG)
        for oh in range(2):
            for hc4 in range(16):
                slot = hc4 % 4
                wsl = self.w2ld[:, slot * 4096:(slot + 1) * 4096]
                src3 = self.w2q_scr[hc4 * 512:(hc4 + 1) * 512,
                                    oh * 1024:(oh + 1) * 1024].rearrange(
                    "(c p) n -> p c n", p=128)
                dst3 = wsl.rearrange("p (c n) -> p c n", c=4)
                self.dma(Q, dst3, src3,
                         need=(f"w2rd{slot}", "w2qst"), tag=f"w2d{slot}")
                for hcl in range(4):
                    hc = hc4 * 4 + hcl
                    for os_ in range(2):
                        first = hc == 0
                        lastm = hc == 63
                        pbk = self.pb[2 + os_]
                        lhs = self.hqfm[:, hc * 128:(hc + 1) * 128]
                        rhs = wsl[:, hcl * 1024 + os_ * 512:
                                  hcl * 1024 + (os_ + 1) * 512]
                        nd = [f"w2d{slot}"] if hcl == 0 else []
                        if first:
                            nd += ["hqf", f"pbfree{2 + os_}"]
                        last_of_slab = hcl == 3
                        self.op(T, lambda e, lhs=lhs, rhs=rhs, pbk=pbk,
                                first=first, lastm=lastm: e.matmul(
                                    pbk[:, 0:512], lhs, rhs, start=first,
                                    stop=lastm),
                                need=tuple(nd),
                                tag=(f"pbmm{2 + os_}" if lastm else None),
                                inc=(self.tP if (last_of_slab or lastm)
                                     else None))
                self.mark(f"w2rd{slot}", self.tP)
            for os_ in range(2):
                pbk = self.pb[2 + os_]
                dst = self.outy[:, oh * 1024 + os_ * 512:
                                oh * 1024 + os_ * 512 + 512]
                self.op(S, lambda e, dst=dst, pbk=pbk: e.activation(
                    dst, pbk[:, 0:512], AF.Identity, scale=cs4),
                    need=(f"pbmm{2 + os_}", "cs4", "outyr"), tag="w2e",
                    inc=self.sP, drain=True)
                self.mark(f"pbfree{2 + os_}", self.sP)
        self.mark("hqfr", self.tP)
        self.dma(Q, self.ybuf[:], self.y_scr[g * 128:(g + 1) * 128, :],
                 need=("ybr", "yst"), tag="yd")
        self.tt(self.outy[:], self.outy[:], self.ybuf[:], OP.add,
                need=("w2e", "yd"), tag="orow", inc=self.vP)
        self.mark("ybr", self.vP)
        self.dma(Q, self.out[g * 128:(g + 1) * 128, :], self.outy[:],
                 need=("orow",), tag="ost")
        self.mark("outyr", self.dQ)
        self.mark("xq3gr", self.tP)

    def fin(self):
        if self.dbg == "yall":
            self.dma("scalar", self.dbg_out[:, :], self.y_scr[:, :],
                     need=("yst",), tag="dbg")
            self.mark("dbg", self.dS)
        for s in (self.dQ, self.dS, self.dG):
            v, h = s.n, s.h
            if v:
                self.ops["sync"].append(
                    lambda e, v=v, h=h: e.wait_ge(h, v))

    def build(self):
        self.alloc()
        self.plan()
        nc = self.nc
        blk = nc.Block()
        block = blk.__enter__()

        @block.sync
        def _(e):
            for f in self.ops["sync"]:
                f(e)

        @block.scalar
        def _(e):
            for f in self.ops["scalar"]:
                f(e)

        @block.vector
        def _(e):
            for f in self.ops["vector"]:
                f(e)

        @block.tensor
        def _(e):
            for f in self.ops["tensor"]:
                f(e)

        @block.gpsimd
        def _(e):
            for f in self.ops["gpsimd"]:
                f(e)

        blk.__exit__(None, None, None)
        while self._ctx:
            self._ctx.pop()[1].__exit__(None, None, None)
        for t in reversed(self._psctx):
            t.__exit__(None, None, None)
        return nc


def stage_inputs(inputs, b):
    qkv_w = np.asarray(inputs["L0_attn_qkv_w"])
    gate_w = np.asarray(inputs["L0_attn_gate_w"])
    proj_w = np.asarray(inputs["L0_attn_proj_w"])
    w1 = np.asarray(inputs["L0_ffn_w1"])
    w2 = np.asarray(inputs["L0_ffn_w2"])

    def chunk_stationary(w):
        O = w.shape[0]
        wt = np.ascontiguousarray(w.T)
        a = wt.reshape(16, 128, O // 128, 128)
        return np.ascontiguousarray(
            a.transpose(2, 1, 0, 3).reshape(O // 128, 128, 2048))

    wqg = np.concatenate([qkv_w, gate_w], axis=0)
    bqg = np.concatenate([np.asarray(inputs["L0_attn_qkv_b"]),
                          np.asarray(inputs["L0_attn_gate_b"])])
    return {
        "x_in": np.ascontiguousarray(np.asarray(inputs["L0_attn_x"])[b]),
        "wqg_in": chunk_stationary(wqg),
        "wp_in": np.ascontiguousarray(proj_w.T.reshape(16, 128, 2048)),
        "w1t_in": np.ascontiguousarray(w1.T),
        "w2t_in": np.ascontiguousarray(w2.T),
        "bqg_in": np.ascontiguousarray(bqg.reshape(64, 128).T),
        "bp_in": np.ascontiguousarray(
            np.asarray(inputs["L0_attn_proj_b"]).reshape(1, -1)),
        "nw_in": np.ascontiguousarray(
            np.asarray(inputs["L0_attn_norm_w"]).reshape(1, -1)),
        "gam_in": np.ascontiguousarray(
            np.asarray(inputs["L0_ffn_gamma"]).reshape(1, -1)),
        "eye_in": np.eye(128).astype(ml_dtypes.bfloat16),
        "mask_in": np.ascontiguousarray(
            (np.arange(128)[:, None] % 8 == np.arange(128)[None, :] % 8)
            .astype(ml_dtypes.bfloat16)),
    }


def kernel(**inputs):
    mk = MK()
    nc = mk.build()
    in_maps = [stage_inputs(inputs, b) for b in range(8)]
    res = run_bass_kernel_spmd(nc, in_maps, core_ids=list(range(8)))
    return np.stack([r["out"] for r in res.results], axis=0)
